# version 19
# baseline (speedup 1.0000x reference)
import contextlib

import numpy as np
import ml_dtypes

import concourse.bass as bass
import concourse.mybir as mybir
from concourse.bass_utils import run_bass_kernel_spmd


F32 = mybir.dt.float32
BF16 = mybir.dt.bfloat16
BF = ml_dtypes.bfloat16
AF = mybir.ActivationFunctionType
AL = mybir.AluOpType
AX = mybir.AxisListType

MAGIC = float(np.float32(1.5 * 2 ** 23))
QMAX = 32767.0
E = 1024
NH = 16
NKV = 8
D = 64
I3 = 3072
NI = I3 // 128
EPS = 1e-5

PAIRS = [(0, 2), (1, 3), (4, 6), (5, 7), (8, 10), (9, 11), (12, 14), (13, 15)]
ENGMAP = {"y": "sync", "v": "vector", "s": "scalar", "t": "tensor", "g": "gpsimd"}


def _slope(h):
    return 2.0 ** (-(h + 1))


class Prog:

    def __init__(self):
        self.steps = []
        self.n = {}
        self.qpar = {}
        self.qlastd = {}

    def cnt(self, sem):
        return self.n.get(sem, 0)

    def bump(self, sem, amt):
        self.n[sem] = self.n.get(sem, 0) + amt
        return self.n[sem]

    def qlast(self, eng):
        return self.qlastd.get(eng, ("q%s0" % eng, 0))

    def step(self, eng, fn, wait=(), inc=None, dma=0):
        w = [(s, v) for s, v in wait if v > 0]
        semn = None
        if dma:
            par = self.qpar.get(eng, 0)
            semn = "q%s%d" % (eng, par)
            pre = self.cnt(semn)
            if pre > 0:
                w.append((semn, pre))
            self.bump(semn, 16 * dma)
            self.qpar[eng] = 1 - par
            self.qlastd[eng] = (semn, self.cnt(semn))
        if inc is not None:
            self.bump(inc[0], inc[1])
        self.steps.append((eng, fn, w, inc, semn))


def build_nc(TOK=2048, taps=()):
    S = 2 * TOK
    NKC = S // 128
    NTM = TOK // 128
    NQ5 = TOK // 512
    taps = tuple(taps)

    def qblocks(p):
        if p == 0 or TOK < 1024:
            return [(i * 512, 512) for i in range(TOK // 512)]
        return [(i * 1024, 1024) for i in range(TOK // 1024)]

    nc = bass.Bass("TRN2", target_bir_lowering=False)

    x_d = nc.dram_tensor("x", [TOK, E], F32, kind="ExternalInput").ap()
    wqt_d = nc.dram_tensor("wqt", [E, E], F32, kind="ExternalInput").ap()
    wkt_d = nc.dram_tensor("wkt", [E, NKV * D], F32, kind="ExternalInput").ap()
    wvt_d = nc.dram_tensor("wvt", [E, NKV * D], F32, kind="ExternalInput").ap()
    wot_d = nc.dram_tensor("wot", [E, E], F32, kind="ExternalInput").ap()
    bo_d = nc.dram_tensor("bo", [128, 8], F32, kind="ExternalInput").ap()
    wup_d = nc.dram_tensor("wup", [I3, E], F32, kind="ExternalInput").ap()
    wgate_d = nc.dram_tensor("wgate", [I3, E], F32, kind="ExternalInput").ap()
    wdown_d = nc.dram_tensor("wdown", [E, I3], F32, kind="ExternalInput").ap()
    out_d = nc.dram_tensor("out", [TOK, E], F32, kind="ExternalOutput").ap()
    poff_d = nc.dram_tensor("poff", [1, 1], mybir.dt.int32,
                            kind="ExternalInput").ap()

    TAPSPEC = {
        "z1": ([TOK, E], BF16), "z3": ([TOK, E], BF16),
        "qt": ([128, 8 * TOK], BF16), "kt": ([128, 4 * S], BF16),
        "vt": ([128, (S // 128) * 512], BF16), "att": ([128, 8 * TOK], BF16),
        "apf": ([E, TOK], BF16), "x1": ([TOK, E], F32),
        "uu": ([I3, TOK], BF16), "gg": ([I3, TOK], BF16),
        "hh": ([128, NI * TOK], BF16), "sca": ([128, 16], F32),
        "bb": ([128, NH], F32), "yy": ([E, TOK], BF16),
        "bias": ([128, 4 * (S // 128)], F32), "wq": ([I3, E], BF16),
    }
    tap_d = {t: nc.dram_tensor("tap_" + t, TAPSPEC[t][0], TAPSPEC[t][1],
                               kind="ExternalOutput").ap() for t in taps}

    z1_d = nc.dram_tensor("z1s", [TOK, E], BF16).ap()
    z3_d = nc.dram_tensor("z3s", [TOK, E], BF16).ap()
    apf_d = nc.dram_tensor("apfs", [E, TOK], BF16).ap()
    yf_d = nc.dram_tensor("yfs", [E, TOK], BF16).ap()
    x1_d = nc.dram_tensor("x1s", [TOK, E], F32).ap()
    u_d = nc.dram_tensor("us", [I3, TOK], BF16).ap()
    g_d = nc.dram_tensor("gs", [I3, TOK], BF16).ap()
    wq_d = {"up": nc.dram_tensor("wupq", [I3, E], BF16).ap(),
            "gate": nc.dram_tensor("wgateq", [I3, E], BF16).ap(),
            "down": nc.dram_tensor("wdownq", [E, I3], BF16).ap()}
    CH = NKV * D * TOK
    kvin = nc.dram_tensor("kvin", [2 * CH], BF16).ap()
    kvout_t = nc.dram_tensor("kvout", [4 * CH], BF16)
    kvout = kvout_t.ap()
    mx_i = nc.dram_tensor("mxi", [1, 256], F32).ap()
    mx_o = nc.dram_tensor("mxo", [1, 256], F32).ap()
    bc_t = nc.dram_tensor("bcs", [64], F32)
    bc_d = bc_t.ap()

    P = Prog()
    regbox = {}
    glob = contextlib.ExitStack()
    with glob:
        def gsb(name, shape, dt=F32):
            return glob.enter_context(nc.sbuf_tensor(name, shape, dt))

        sems = {}
        for nm in ("qy0", "qy1", "qs0", "qs1", "qg0", "qg1",
                   "cv", "cs", "ct", "cg", "cc"):
            sems[nm] = glob.enter_context(nc.semaphore(nm))
        sems["qy"] = sems["qy0"]
        sems["qs"] = sems["qs0"]
        sems["qg"] = sems["qg0"]

        iq = gsb("iq", [128, 2048])
        ikg = gsb("ikg", [128, NKC])
        ones64 = gsb("ones64", [128, 64], BF16)
        ones2 = gsb("ones2", [128, 65], BF16)
        bo_s = gsb("bo_s", [128, 8])
        bball = gsb("bball", [128, NH])
        sca = gsb("sca", [128, 16])
        scr1 = gsb("scr1", [1, 32])
        biasm = gsb("biasm", [128, 8 * NKC])
        mqk = gsb("mqk", [128, 16])
        mqkred = gsb("mqkred", [128, 1])
        mq64 = gsb("mq64", [1, 16])
        ffmax = gsb("ffmax", [128, 2 * NI * (TOK // 512)])
        poff_s = gsb("poff_s", [1, 1], mybir.dt.int32)
        clrow = gsb("clrow", [1, 256])
        mgcb = gsb("mgcb", [128, 1])
        parb = gsb("parb", [128, 4])

        def f(g):
            g.iota(iq[:], [[1, 2048]], channel_multiplier=0,
                   allow_small_or_imprecise_dtypes=True)
            g.iota(ikg[:], [[128, NKC]], channel_multiplier=1,
                   allow_small_or_imprecise_dtypes=True)
            r = g.alloc_register()
            regbox["partner"] = r
            r2 = g.alloc_register()
            regbox["tmp"] = r2
            g.dma_start(out=poff_s[:], in_=poff_d).then_inc(sems["qg"], 16)
            g.wait_ge(sems["qg"], 16)
            g.reg_load(r, poff_s[0:1, 0:1])
            return g.engine_nop()
        P.step("g", f, inc=("cg", 1), dma=1)

        def f(v):
            v.memset(ones64[:], 1.0)
            v.memset(ones2[:], 0.0)
            v.memset(mgcb[:], MAGIC)
            v.drain()
            v.memset(ones2[0:64, 0:1], 1.0)
            v.memset(ones2[64:128, 64:65], 1.0)
            return v.drain()
        P.step("v", f, inc=("cv", 1))

        def f(y):
            return y.dma_start(out=bo_s[:], in_=bo_d).then_inc(sems["qy"], 16)
        P.step("y", f, dma=1)


        def phase_barrier():
            w = [("cs", P.cnt("cs")), ("ct", P.cnt("ct")),
                 ("cg", P.cnt("cg")), ("cc", P.cnt("cc"))]
            for qn in ("qy0", "qy1", "qs0", "qs1", "qg0", "qg1"):
                if P.cnt(qn) > 0:
                    w.append((qn, P.cnt(qn)))

            def fb(v):
                return None
            P.step("v", fb, wait=w, inc=("cv", 1))
            bar = P.cnt("cv")
            for en in ("s", "t", "g", "y"):
                def fb2(e):
                    return None
                P.step(en, fb2, wait=[("cv", bar)])

        attsc = contextlib.ExitStack()
        qt_s = attsc.enter_context(nc.sbuf_tensor("qt_s", [128, 8 * TOK], BF16))
        kt_s = attsc.enter_context(nc.sbuf_tensor("kt_s", [128, 4 * S], BF16))
        vt_s = attsc.enter_context(nc.sbuf_tensor("vt_s", [128, NKC * 512], BF16))

        pA = contextlib.ExitStack()

        def asb(name, shape, dt=F32):
            return pA.enter_context(nc.sbuf_tensor(name, shape, dt))

        xr = asb("xr", [128, 4 * E])
        z1r = asb("z1r", [128, 2 * E], BF16)
        mv2 = asb("mv2", [128, 2])
        st12 = asb("st12", [128, 12])
        rr = asb("rr", [128, NTM])
        mbb = asb("mbb", [128, NTM])
        wst = asb("wst", [128, 2 * E])
        wpb = asb("wpb", [128, 8 * E], BF16)
        zf = asb("zf", [128, 8 * TOK], BF16)
        sqq = asb("sqq", [128, 512], BF16)
        pmA = [pA.enter_context(nc.psum_tensor("pmA%d" % i, [128, 512], F32))
               for i in range(2)]

        norm_cs = {}
        z1w_qs = {}
        for i in range(NTM):
            xi = xr[:, (i % 4) * E:(i % 4) * E + E]

            def f(y, xi=xi, i=i):
                return y.dma_start(out=xi, in_=x_d[i * 128:(i + 1) * 128, :]
                                   ).then_inc(sems["qy"], 16)
            P.step("y", f, wait=([("cs", norm_cs[i - 4])] if i >= 4 else []),
                   dma=1)

            def f(v, xi=xi, i=i):
                v.bn_stats(st12[:, 0:6], xi[:, 0:512])
                v.bn_stats(st12[:, 6:12], xi[:, 512:1024])
                v.drain()
                v.bn_aggr(mv2[:], st12[:])
                v.drain()
                return v.tensor_scalar(rr[:, i:i + 1], mv2[:, 1:2], EPS,
                                       None, AL.add)
            P.step("v", f, wait=[P.qlast("y")], inc=("cv", 1))

            def f(s, i=i):
                return s.activation(rr[:, i:i + 1], rr[:, i:i + 1], AF.Sqrt,
                                    bias=0.0, scale=1.0)
            P.step("s", f, wait=[("cv", P.cnt("cv"))], inc=("cs", 1))

            def f(v, i=i):
                v.reciprocal(rr[:, i:i + 1], rr[:, i:i + 1])
                v.drain()
                return v.scalar_tensor_tensor(mbb[:, i:i + 1], mv2[:, 0:1],
                                              -1.0, rr[:, i:i + 1], AL.mult,
                                              AL.mult)
            P.step("v", f, wait=[("cs", P.cnt("cs"))], inc=("cv", 1))

            zsl = z1r[:, (i % 2) * E:(i % 2) * E + E]

            def f(s, i=i, xi=xi, zsl=zsl):
                s.activation(zsl, xi, AF.Identity, bias=mbb[:, i:i + 1],
                             scale=rr[:, i:i + 1])
                s.drain()
                s.dma_start(out=z1_d[i * 128:(i + 1) * 128, :], in_=zsl
                            ).then_inc(sems["qs"], 16)
                return None
            P.step("s", f, wait=[("cv", P.cnt("cv"))] +
                   ([z1w_qs[i - 2]] if i >= 2 else []),
                   inc=("cs", 1), dma=1)
            norm_cs[i] = P.cnt("cs")
            z1w_qs[i] = P.qlast("s")

        def f(s):
            last = None
            for c in range(8):
                last = s.dma_start(out=zf[:, c * TOK:(c + 1) * TOK],
                                   in_=z1_d[:, c * 128:(c + 1) * 128],
                                   transpose=True).then_inc(sems["qs"], 16)
            return last
        P.step("s", f, wait=[P.qlast("s")], dma=8)
        ck_zf = P.qlast("s")

        if "z1" in taps:
            def f(g):
                return g.dma_start(out=tap_d["z1"][:, :], in_=z1_d[:, :]
                                   ).then_inc(sems["qg"], 16)
            P.step("g", f, wait=[ck_zf], dma=1)

        wq_cv = {}
        for i in range(8):
            def f(y, i=i):
                return y.dma_start(out=wst[:, (i % 2) * E:(i % 2) * E + E],
                                   in_=wqt_d[i * 128:(i + 1) * 128, :]
                                   ).then_inc(sems["qy"], 16)
            P.step("y", f, wait=([("cv", wq_cv[i - 2])] if i >= 2 else []),
                   dma=1)

            def f(v, i=i):
                return v.tensor_copy(wpb[:, E * i:E * i + E],
                                     wst[:, (i % 2) * E:(i % 2) * E + E])
            P.step("v", f, wait=[P.qlast("y")], inc=("cv", 1))
            wq_cv[i] = P.cnt("cv")
        ck_wq = P.cnt("cv")

        evac_cv = {}
        nmm = 0
        for o in range(8):
            for qc in range(NQ5):
                bank = pmA[nmm % 2]

                def f(t, o=o, qc=qc, bank=bank):
                    last = None
                    for fc in range(8):
                        last = t.matmul(
                            bank[:],
                            wpb[:, E * fc + 128 * o:E * fc + 128 * o + 128],
                            zf[:, fc * TOK + qc * 512:fc * TOK + qc * 512 + 512],
                            start=(fc == 0), stop=(fc == 7))
                    return last
                w = [("cv", ck_wq), ck_zf]
                if nmm >= 2:
                    w.append(("cv", evac_cv[nmm - 2]))
                P.step("t", f, wait=w, inc=("ct", 1))

                def f(v, o=o, qc=qc, bank=bank):
                    return v.tensor_copy(
                        qt_s[:, o * TOK + qc * 512:o * TOK + qc * 512 + 512],
                        bank[:])
                P.step("v", f, wait=[("ct", P.cnt("ct"))], inc=("cv", 1))
                evac_cv[nmm] = P.cnt("cv")
                nmm += 1
        ck_qproj_t = P.cnt("ct")

        wkv_cv = {}
        for i in range(16):
            src = wkt_d if i < 8 else wvt_d
            fc = i % 8
            dst = 512 * i

            def f(y, i=i, src=src, fc=fc):
                return y.dma_start(out=wst[:, (i % 2) * E:(i % 2) * E + 512],
                                   in_=src[fc * 128:(fc + 1) * 128, :]
                                   ).then_inc(sems["qy"], 16)
            P.step("y", f, wait=[("ct", ck_qproj_t)] +
                   ([("cv", wkv_cv[i - 2])] if i >= 2 else []), dma=1)

            def f(v, i=i, dst=dst):
                return v.tensor_copy(wpb[:, dst:dst + 512],
                                     wst[:, (i % 2) * E:(i % 2) * E + 512])
            P.step("v", f, wait=[P.qlast("y")], inc=("cv", 1))
            wkv_cv[i] = P.cnt("cv")
        ck_wkv = P.cnt("cv")

        for u in range(4):
            for qc in range(NQ5):
                bank = pmA[nmm % 2]

                def f(t, u=u, qc=qc, bank=bank):
                    last = None
                    for fc in range(8):
                        last = t.matmul(
                            bank[:],
                            wpb[:, 512 * fc + 128 * u:512 * fc + 128 * u + 128],
                            zf[:, fc * TOK + qc * 512:fc * TOK + qc * 512 + 512],
                            start=(fc == 0), stop=(fc == 7))
                    return last
                P.step("t", f, wait=[("cv", max(ck_wkv, evac_cv[nmm - 2]))],
                       inc=("ct", 1))

                def f(v, u=u, qc=qc, bank=bank):
                    return v.tensor_copy(
                        kt_s[:, u * S + qc * 512:u * S + qc * 512 + 512],
                        bank[:])
                P.step("v", f, wait=[("ct", P.cnt("ct"))], inc=("cv", 1))
                evac_cv[nmm] = P.cnt("cv")
                nmm += 1

        for tc in range(NTM):
            bank = pmA[nmm % 2]

            def f(t, tc=tc, bank=bank):
                last = None
                for fc in range(8):
                    last = t.matmul(
                        bank[:],
                        zf[:, fc * TOK + tc * 128:fc * TOK + tc * 128 + 128],
                        wpb[:, 512 * (8 + fc):512 * (8 + fc) + 512],
                        start=(fc == 0), stop=(fc == 7))
                return last
            P.step("t", f, wait=[("cv", max(ck_wkv, evac_cv[nmm - 2]))],
                   inc=("ct", 1))

            def f(v, tc=tc, bank=bank):
                return v.tensor_copy(vt_s[:, tc * 512:tc * 512 + 512], bank[:])
            P.step("v", f, wait=[("ct", P.cnt("ct"))], inc=("cv", 1))
            evac_cv[nmm] = P.cnt("cv")
            nmm += 1
        ck_kvproj = P.cnt("cv")

        def f(g):
            last = None
            for u in range(4):
                last = g.dma_start(
                    out=kvin[u * 128 * TOK:(u + 1) * 128 * TOK].rearrange(
                        "(p t) -> p t", p=128),
                    in_=kt_s[:, u * S:u * S + TOK]).then_inc(sems["qg"], 16)
            for tc in range(NTM):
                last = g.dma_start(
                    out=kvin[CH + tc * 128 * 512:CH + (tc + 1) * 128 * 512
                             ].rearrange("(p t) -> p t", p=128),
                    in_=vt_s[:, tc * 512:tc * 512 + 512]).then_inc(sems["qg"], 16)
            return last
        P.step("g", f, wait=[("cv", ck_kvproj)], dma=4 + NTM)

        def f(g):
            return g.collective_compute(
                "AllGather", AL.bypass,
                replica_groups=[[0, 1], [2, 3], [4, 5], [6, 7]],
                ins=[kvin], outs=[kvout.rearrange("(r n) -> r n", r=2)])
        P.step("g", f, wait=[P.qlast("g")], inc=("cc", 1))
        ck_ag = P.cnt("cc")

        for u in range(4):
            def f(g, u=u):
                r, r2 = regbox["partner"], regbox["tmp"]
                g.reg_mov(r2, u * 128 * TOK)
                g.reg_alu(r2, r2, r, AL.add)
                src = bass.AP(kvout_t, r2, [[TOK, 128], [1, TOK]])
                return g.dma_start(out=kt_s[:, u * S + TOK:u * S + S],
                                   in_=src).then_inc(sems["qg"], 16)
            P.step("g", f, wait=[("cc", ck_ag)], dma=1)
        for tc in range(NTM):
            def f(g, tc=tc):
                r, r2 = regbox["partner"], regbox["tmp"]
                g.reg_mov(r2, CH + tc * 128 * 512)
                g.reg_alu(r2, r2, r, AL.add)
                src = bass.AP(kvout_t, r2, [[512, 128], [1, 512]])
                return g.dma_start(
                    out=vt_s[:, (NTM + tc) * 512:(NTM + tc + 1) * 512],
                    in_=src).then_inc(sems["qg"], 16)
            P.step("g", f, wait=[("cc", ck_ag)], dma=1)
        ck_kv_g = P.qlast("g")
        ck_kv_v = P.cnt("cv")

        if "kt" in taps:
            def f(g):
                return g.dma_start(out=tap_d["kt"][:, :], in_=kt_s[:]
                                   ).then_inc(sems["qg"], 16)
            P.step("g", f, wait=[("cv", ck_kv_v)], dma=1)
        if "vt" in taps:
            def f(g):
                return g.dma_start(out=tap_d["vt"][:, :], in_=vt_s[:]
                                   ).then_inc(sems["qg"], 16)
            P.step("g", f, wait=[ck_kv_g], dma=1)
        if "qt" in taps:
            def f(g):
                return g.dma_start(out=tap_d["qt"][:, :], in_=qt_s[:]
                                   ).then_inc(sems["qg"], 16)
            P.step("g", f, wait=[("cv", P.cnt("cv"))], dma=1)

        def emit_norm(src, colbase, width, dst_col, part_lo=None):
            for ch in range(width // 512):
                if part_lo is not None and ch >= part_lo:
                    wr = [("cv", ck_kv_v), ck_kv_g]
                else:
                    wr = [("cv", ck_kvproj)]
                def f(s, src=src, colbase=colbase, ch=ch):
                    s.activation(sqq[:],
                                 src[:, colbase + ch * 512:colbase + ch * 512 + 512],
                                 AF.Square, bias=0.0, scale=1.0)
                    return s.drain()
                P.step("s", f, wait=wr + [("ct", P.cnt("ct")),
                                          ("cv", P.cnt("cv"))],
                       inc=("cs", 1))

                def f(t):
                    return t.matmul(pmA[0][0:65, :], ones2[:], sqq[:],
                                    start=True, stop=True)
                P.step("t", f, wait=[("cs", P.cnt("cs"))], inc=("ct", 1))

                def f(v, dst_col=dst_col, ch=ch):
                    v.tensor_reduce(mqkred[0:65, 0:1], pmA[0][0:65, :], AX.X,
                                    AL.max)
                    v.drain()
                    if ch == 0:
                        v.tensor_copy(mqk[0:65, dst_col:dst_col + 1],
                                      mqkred[0:65, 0:1])
                    else:
                        v.tensor_tensor(mqk[0:65, dst_col:dst_col + 1],
                                        mqk[0:65, dst_col:dst_col + 1],
                                        mqkred[0:65, 0:1], AL.max)
                    return v.drain()
                P.step("v", f, wait=[("ct", P.cnt("ct"))], inc=("cv", 1))

        for p in range(8):
            emit_norm(qt_s, p * TOK, TOK, p)
        for u in range(4):
            emit_norm(kt_s, u * S, S, 8 + u, part_lo=TOK // 512)

        def f(v):
            v.tensor_copy(mq64[0:1, 0:12], mqk[64:65, 0:12])
            v.drain()
            for p in range(8):
                for j in range(2):
                    h = PAIRS[p][j]
                    kv = h // 2
                    u, j2 = kv // 2, kv % 2
                    a = mqk[0:1, p:p + 1] if j == 0 else mq64[0:1, p:p + 1]
                    b = (mqk[0:1, 8 + u:9 + u] if j2 == 0
                         else mq64[0:1, 8 + u:9 + u])
                    v.tensor_tensor(scr1[0:1, h:h + 1], a, b, AL.mult)
            return v.drain()
        P.step("v", f, wait=[("cv", P.cnt("cv"))], inc=("cv", 1))

        def f(s):
            s.activation(scr1[0:1, 0:16], scr1[0:1, 0:16], AF.Sqrt,
                         bias=0.0, scale=1.0)
            return s.drain()
        P.step("s", f, wait=[("cv", P.cnt("cv"))], inc=("cs", 1))

        def f(v):
            v.tensor_scalar(scr1[0:1, 0:16], scr1[0:1, 0:16], 1.0 / 8.0, None,
                            AL.mult)
            return v.drain()
        P.step("v", f, wait=[("cs", P.cnt("cs"))], inc=("cv", 1))

        def f(g):
            return g.dma_start(out=bc_d[0:16].rearrange("(o n) -> o n", o=1),
                               in_=scr1[0:1, 0:16]).then_inc(sems["qg"], 16)
        P.step("g", f, wait=[("cv", P.cnt("cv"))], dma=1)

        def f(g):
            return g.dma_start(out=bball[:],
                               in_=bass.AP(bc_t, 0, [[0, 128], [1, 16]])
                               ).then_inc(sems["qg"], 16)
        P.step("g", f, wait=[P.qlast("g")], dma=1)

        def f(g):
            return g.engine_nop()
        P.step("g", f, wait=[P.qlast("g")], inc=("cg", 1))
        ck_bball = P.cnt("cg")

        if "bb" in taps:
            def f(g):
                return g.dma_start(out=tap_d["bb"][:, :], in_=bball[:]
                                   ).then_inc(sems["qg"], 16)
            P.step("g", f, wait=[("cg", ck_bball)], dma=1)

        pA.close()
        phase_barrier()

        pC = contextlib.ExitStack()

        def csb(name, shape, dt=F32):
            return pC.enter_context(nc.sbuf_tensor(name, shape, dt))

        att = csb("att", [128, 8 * TOK], BF16)
        wob = csb("wob", [128, 8 * E], BF16)
        lpq = csb("lpq", [128, 2 * 1024])
        lps = csb("lps", [128, 4 * 48])
        lpr = csb("lpr", [128, 4 * 48])
        lpb = csb("lpb", [128, 2 * 1024], BF16)
        apr = csb("apr", [128, 2 * 512], BF16)

        pC1 = contextlib.ExitStack()

        def c1sb(name, shape, dt=F32):
            return pC1.enter_context(nc.sbuf_tensor(name, shape, dt))

        QLM = 1024 if TOK >= 1024 else 512
        sc_b = [c1sb("scb%d" % j, [128, 2 * QLM], BF16) for j in range(2)]
        tA = c1sb("tA", [128, 2 * 512], BF16)
        tev = c1sb("tev", [128, 2 * 512])
        rcp = c1sb("rcp", [128, 512])
        prc = c1sb("prc", [128, 2])
        gdg = c1sb("gdg", [128, 2 * (2 * QLM - 128)], BF16)
        gdgf = c1sb("gdgf", [128, 2 * (2 * QLM - 128)], BF16)
        gev = c1sb("gev", [128, 2 * QLM], BF16)
        pe3 = [pC1.enter_context(nc.psum_tensor("pe%d" % i, [128, QLM], F32))
               for i in range(3)]
        pvb = [[pC1.enter_context(
            nc.psum_tensor("pv%d%d" % (j, hh), [128, 512], F32))
            for hh in range(QLM // 512)] for j in range(1)]

        def sc_ap(sslot, off, width):
            return sc_b[sslot // 2][:, (sslot % 2) * QLM + off:
                                    (sslot % 2) * QLM + off + width]

        bg_work = []
        nbg = 0
        for kind, n_t, blkw in (("up", NI, E), ("gate", NI, E),
                                ("down", 8, I3)):
            src = {"up": wup_d, "gate": wgate_d, "down": wdown_d}[kind]
            nsub = blkw // 1024 if blkw > 1024 else 1
            subw = min(blkw, 1024)
            for ti in range(n_t):
                for si in range(nsub):
                    q2 = (nbg % 2) * 1024
                    s4 = (nbg % 4) * 48
                    nbg += 1

                    def fl(y, src=src, ti=ti, si=si, subw=subw, q2=q2):
                        return y.dma_start(
                            out=lpq[:, q2:q2 + subw],
                            in_=src[ti * 128:(ti + 1) * 128,
                                    si * subw:(si + 1) * subw]
                            ).then_inc(sems["qy"], 16)

                    def fd(v, subw=subw, q2=q2, s4=s4):
                        nblk = subw // 32
                        sl3 = lpq[:, q2:q2 + subw].rearrange(
                            "p (b j) -> p b j", j=32)
                        v.tensor_reduce(lps[:, s4:s4 + nblk], sl3, AX.X,
                                        AL.max, apply_absolute_value=True)
                        v.drain()
                        v.tensor_scalar(lps[:, s4:s4 + nblk],
                                        lps[:, s4:s4 + nblk],
                                        1.0 / 7.0, 1e-12, AL.mult, AL.max)
                        v.drain()
                        v.reciprocal(lpr[:, s4:s4 + nblk],
                                     lps[:, s4:s4 + nblk])
                        v.drain()
                        rb = bass.AP(lpr, s4, [[192, 128], [1, nblk], [0, 32]])
                        v.tensor_tensor(sl3, sl3, rb, AL.mult)
                        v.drain()
                        v.tensor_scalar(sl3, sl3, MAGIC, MAGIC, AL.add,
                                        AL.subtract)
                        v.drain()
                        sb3 = lpb[:, q2:q2 + subw].rearrange(
                            "p (b j) -> p b j", j=32)
                        scb = bass.AP(lps, s4, [[192, 128], [1, nblk], [0, 32]])
                        v.tensor_tensor(sb3, sl3, scb, AL.mult)
                        return v.drain()

                    def fw(g, kind=kind, ti=ti, si=si, subw=subw, q2=q2):
                        return g.dma_start(
                            out=wq_d[kind][ti * 128:(ti + 1) * 128,
                                           si * subw:(si + 1) * subw],
                            in_=lpb[:, q2:q2 + subw]).then_inc(sems["qg"], 16)
                    bg_work.append(("lp", fl, fd, fw))
        for i in range(8):
            q2 = (nbg % 2) * 1024
            nbg += 1

            def fl(y, i=i, q2=q2):
                return y.dma_start(out=lpq[:, q2:q2 + 1024],
                                   in_=wot_d[i * 128:(i + 1) * 128, :]
                                   ).then_inc(sems["qy"], 16)

            def fc(v, i=i, q2=q2):
                v.tensor_copy(wob[:, E * i:E * i + E], lpq[:, q2:q2 + 1024])
                return v.drain()
            bg_work.append(("wo", fl, fc))

        bg_iter = iter(bg_work)
        bg_state = {"n": 0,
                    "lpq": [None, None],
                    "lps": [None, None, None, None],
                    "lpb": [None, None]}

        def emit_bg(nslots):
            for _ in range(nslots):
                it = next(bg_iter, None)
                if it is None:
                    return
                n = bg_state["n"]
                bg_state["n"] = n + 1
                sq = n % 2
                s4i = n % 4
                if it[0] == "lp":
                    _, fl, fd, fw = it
                    wl = ([bg_state["lpq"][sq]]
                          if bg_state["lpq"][sq] else [])
                    P.step("y", fl, wait=wl, dma=1)
                    wv = [P.qlast("y")]
                    if bg_state["lpb"][sq] is not None:
                        wv.append(bg_state["lpb"][sq])
                    P.step("v", fd, wait=wv, inc=("cv", 1))
                    bg_state["lpq"][sq] = ("cv", P.cnt("cv"))
                    P.step("g", fw, wait=[("cv", P.cnt("cv"))], dma=1)
                    bg_state["lpb"][sq] = P.qlast("g")
                else:
                    _, fl, fc = it
                    wl = ([bg_state["lpq"][sq]]
                          if bg_state["lpq"][sq] else [])
                    P.step("y", fl, wait=wl, dma=1)
                    P.step("v", fc, wait=[P.qlast("y")], inc=("cv", 1))
                    bg_state["lpq"][sq] = ("cv", P.cnt("cv"))


        exp_done = {}
        pv_done = {}
        ev_prev = 0
        ev_bank = 0
        HL = [(p, j) for p in range(8) for j in range(2)]
        pg_done = {}

        def emit_pairgen(idx):
            p, j = HL[idx]
            h = PAIRS[p][j]
            sl = _slope(h)
            qbl = qblocks(p)
            QL = qbl[0][1]
            D0 = QL - 128
            pr2 = idx % 2
            gbase = pr2 * (2 * QLM - 128)
            vbase = pr2 * QLM
            bbase = pr2 * (4 * NKC)
            w0 = [("cg", ck_bball), ("cv", ev_prev),
                  ("cs", exp_done.get("last", 0))]

            def f(v, h=h, sl=sl, qbl=qbl, bbase=bbase):
                last = None
                for qbi, (q0, ql) in enumerate(qbl):
                    bnd = min(NKC // 2, (q0 + ql) // 128)
                    base = bbase + qbi * NKC
                    slices = ((0, bnd, float(-q0), 1.0),
                              (bnd, NKC // 2, float(-q0), -1.0),
                              (NKC // 2, NKC,
                               float(q0 - (S - 1 + TOK)), 1.0))
                    for lo, hi, c0, sgn in slices:
                        if lo >= hi:
                            continue
                        v.tensor_scalar(biasm[:, base + lo:base + hi],
                                        ikg[:, lo:hi], c0,
                                        float(sgn * sl / 8.0),
                                        AL.add, AL.mult)
                        v.drain()
                        last = v.tensor_scalar(
                            biasm[:, base + lo:base + hi],
                            biasm[:, base + lo:base + hi],
                            bball[:, h:h + 1], None, AL.subtract)
                        v.drain()
                return last
            P.step("v", f, wait=w0, inc=("cv", 1))

            def f(v, sl=sl, D0=D0, pr2=pr2):
                return v.tensor_scalar(prc[:, pr2:pr2 + 1], ikg[:, 0:1],
                                       float(D0), float(-2.0 * sl / 8.0),
                                       AL.add, AL.mult)
            P.step("v", f, wait=[("cv", P.cnt("cv"))], inc=("cv", 1))

            def f(s, sl=sl, QL=QL, pr2=pr2, gbase=gbase, vbase=vbase):
                s.activation(gdgf[:, gbase:gbase + 2 * QL - 128],
                             iq[:, 0:2 * QL - 128], AF.Exp,
                             bias=prc[:, pr2:pr2 + 1], scale=2.0 * sl / 8.0)
                s.drain()
                return s.activation(gev[:, vbase:vbase + QL], iq[:, 0:QL],
                                    AF.Exp, bias=0.0, scale=2.0 * sl / 8.0)
            P.step("s", f, wait=[("cv", P.cnt("cv"))], inc=("cs", 1))

            def f(v, QL=QL, gbase=gbase):
                return v.tensor_scalar(gdg[:, gbase:gbase + 2 * QL - 128],
                                       gdgf[:, gbase:gbase + 2 * QL - 128],
                                       1.0, 1.0, AL.mult, AL.min)
            P.step("v", f, wait=[("cs", P.cnt("cs"))], inc=("cv", 1))
            pg_done[idx] = P.cnt("cv")

        for idx, (p, j) in enumerate(HL):
            if idx == 0:
                emit_pairgen(0)
            if idx + 1 < len(HL):
                emit_pairgen(idx + 1)
            if True:
                h = PAIRS[p][j]
                u = p // 2
                g_kv = 2 * u + j
                sl = _slope(h)
                qbl = qblocks(p)
                QL = qbl[0][1]
                D0 = QL - 128
                jb = 64 * j
                pr2 = idx % 2
                gbase = pr2 * (2 * QLM - 128)
                vbase = pr2 * QLM
                bbase = pr2 * (4 * NKC)
                ck_pairgen = pg_done[idx]

                for qbi, (q0, ql) in enumerate(qbl):
                    bnd = min(NKC // 2, (q0 + ql) // 128)
                    d0c = q0 // 128
                    nq5 = ql // 512
                    kcs = list(range(NKC))
                    nkl = len(kcs)
                    pend_pv = None
                    for ki in range(nkl + 1):
                      if ki < nkl:
                        kc = kcs[ki]
                        reg = "A" if kc < bnd else "B"
                        isdiag = (d0c <= kc < bnd)
                        eset = ki % 3
                        sslot = ki % 4

                        def f(t, u=u, kc=kc, q0=q0, nq5=nq5, p=p, jb=jb,
                              eset=eset):
                            last = None
                            for hf in range(nq5):
                                last = t.matmul(
                                    pe3[eset][:, hf * 512:hf * 512 + 512],
                                    kt_s[jb:jb + 64,
                                         u * S + kc * 128:u * S + kc * 128 + 128],
                                    qt_s[jb:jb + 64,
                                         p * TOK + q0 + hf * 512:
                                         p * TOK + q0 + hf * 512 + 512],
                                    start=True, stop=True,
                                    tile_position=(jb, 0))
                            return last
                        w = [("cv", ck_pairgen)]
                        if ki >= 3:
                            w.append(("cs", exp_done[(qbi, kcs[ki - 3])]))
                        elif "last" in exp_done:
                            w.append(("cs", exp_done["last"]))
                        P.step("t", f, wait=w, inc=("ct", 1))
                        ck_energy = P.cnt("ct")

                      if pend_pv is not None:
                          fn_pv, w_pv, kprev = pend_pv
                          P.step("t", fn_pv, wait=w_pv, inc=("ct", 1))
                          pv_done[(qbi, kprev)] = P.cnt("ct")
                          pv_done["last"] = P.cnt("ct")
                          pend_pv = None
                      if ki == nkl:
                          break

                      def f(s, kc=kc, ql=ql, qbi=qbi, sslot=sslot,
                            eset=eset, bbase=bbase):
                          return s.activation(
                              sc_ap(sslot, 0, ql),
                              pe3[eset][:, 0:ql], AF.Exp,
                              bias=biasm[:, bbase + qbi * NKC + kc:
                                         bbase + qbi * NKC + kc + 1],
                              scale=0.125)
                      w = [("ct", ck_energy)]
                      if ki >= 4:
                          w.append(("ct", pv_done[(qbi, kcs[ki - 4])]))
                      elif "last" in pv_done:
                          w.append(("ct", pv_done["last"]))
                      P.step("s", f, wait=w, inc=("cs", 1))
                      exp_done[(qbi, kc)] = P.cnt("cs")
                      exp_done["last"] = P.cnt("cs")

                      ck_corr = None
                      if isdiag:
                          off = D0 - (kc * 128 - q0)

                          def f(v, off=off, ql=ql, sslot=sslot, gbase=gbase):
                              return v.tensor_tensor(
                                  sc_ap(sslot, 0, ql),
                                  sc_ap(sslot, 0, ql),
                                  gdg[:, gbase + off:gbase + off + ql],
                                  AL.mult)
                          P.step("v", f, wait=[("cs", exp_done[(qbi, kc)])],
                                 inc=("cv", 1))
                          ck_corr = P.cnt("cv")

                      first_A = (ki == 0)
                      last_A = (kc == bnd - 1)
                      first_B = (ki == bnd)
                      last_B = (kc == NKC - 1)

                      ck_snap = None
                      if reg == "B" and first_B:
                          for hf in range(nq5):
                              def f(v, hf=hf):
                                  return v.tensor_copy(
                                      tA[:, hf * 512:hf * 512 + 512],
                                      pvb[0][hf][:])
                              P.step("v", f,
                                     wait=[("ct", pv_done[(qbi, kcs[ki - 1])])],
                                     inc=("cv", 1))
                          ck_snap = P.cnt("cv")

                      def f(t, g_kv=g_kv, kc=kc, nq5=nq5, sslot=sslot,
                            reg=reg, first_A=first_A, last_A=last_A,
                            first_B=first_B, last_B=last_B):
                          st = first_A if reg == "A" else first_B
                          sp = last_A if reg == "A" else last_B
                          last = None
                          for hf in range(nq5):
                              rhs = sc_ap(sslot, hf * 512, 512)
                              bank = pvb[0][hf]
                              last = t.matmul(
                                  bank[0:64, :],
                                  vt_s[:, kc * 512 + g_kv * 64:
                                       kc * 512 + g_kv * 64 + 64],
                                  rhs, start=st, stop=sp,
                                  tile_position=(0, 0),
                                  skip_group_check=True)
                              last = t.matmul(
                                  bank[64:128, :], ones64[:], rhs,
                                  start=st, stop=sp, tile_position=(0, 64),
                                  skip_group_check=True)
                          return last
                      w = [("cs", exp_done[(qbi, kc)])]
                      if ck_corr is not None:
                          w.append(("cv", ck_corr))
                      if ck_snap is not None:
                          w.append(("cv", ck_snap))
                      if first_A:
                          w.append(("cv", ev_bank))
                      pend_pv = (f, w, kc)

                    for hf in range(nq5):
                        bank = pvb[0][hf]
                        tsl = tev[:, hf * 512:hf * 512 + 512]
                        asl = tA[:, hf * 512:hf * 512 + 512]

                        def f(v, hf=hf, bank=bank, tsl=tsl, asl=asl,
                              vbase=vbase):
                            v.scalar_tensor_tensor(
                                tsl, bank[:], 1.0,
                                gev[:, vbase + hf * 512:vbase + hf * 512 + 512],
                                AL.mult, AL.mult)
                            v.drain()
                            return v.tensor_tensor(tsl, tsl, asl, AL.add)
                        P.step("v", f, wait=[("ct", pv_done[(qbi, NKC - 1)])],
                               inc=("cv", 1))

                        def f(s, hf=hf, tsl=tsl):
                            s.activation(rcp[0:64, 0:512],
                                         tsl[64:128, :], AF.Ln, bias=0.0,
                                         scale=1.0)
                            s.drain()
                            return s.activation(
                                rcp[0:64, 0:512],
                                rcp[0:64, 0:512],
                                AF.Exp, bias=0.0, scale=-1.0)
                        P.step("s", f, wait=[("cv", P.cnt("cv"))], inc=("cs", 1))

                        def f(v, p=p, jb=jb, hf=hf, q0=q0, tsl=tsl):
                            return v.tensor_tensor(
                                att[jb:jb + 64,
                                    p * TOK + q0 + hf * 512:
                                    p * TOK + q0 + hf * 512 + 512],
                                tsl[0:64, :],
                                rcp[0:64, 0:512], AL.mult)
                        P.step("v", f, wait=[("cs", P.cnt("cs"))], inc=("cv", 1))
                    ev_prev = P.cnt("cv")
                    ev_bank = P.cnt("cv") - 1
                    emit_bg(2)

        ck_att_v = P.cnt("cv")

        if "att" in taps:
            def f(g):
                return g.dma_start(out=tap_d["att"][:, :], in_=att[:]
                                   ).then_inc(sems["qg"], 16)
            P.step("g", f, wait=[("cv", ck_att_v)], dma=1)

        while True:
            before = len(P.steps)
            emit_bg(4)
            if len(P.steps) == before:
                break
        ck_bg_qy = P.qlast("g")
        ck_bg_cv = P.cnt("cv")

        if "wq" in taps:
            def f(g):
                return g.dma_start(out=tap_d["wq"][:, :], in_=wq_d["up"][:, :]
                                   ).then_inc(sems["qg"], 16)
            P.step("g", f, wait=[ck_bg_qy], dma=1)

        pC1.close()
        phase_barrier()

        pD = contextlib.ExitStack()

        def dsb(name, shape, dt=F32):
            return pD.enter_context(nc.sbuf_tensor(name, shape, dt))

        apev = {}
        apw = {}
        ck_apf_qc = {}
        nmm2 = 0
        poA = pD.enter_context(nc.psum_tensor("poA", [128, 512], F32))
        poB = pD.enter_context(nc.psum_tensor("poB", [128, 512], F32))
        for qc in range(NQ5):
            for o in range(8):
                bank = (poA, poB)[nmm2 % 2]

                def f(t, o=o, qc=qc, bank=bank):
                    last = None
                    for fc in range(8):
                        last = t.matmul(
                            bank[:],
                            wob[:, E * fc + 128 * o:E * fc + 128 * o + 128],
                            att[:, fc * TOK + qc * 512:fc * TOK + qc * 512 + 512],
                            start=(fc == 0), stop=(fc == 7))
                    return last
                w = [("cv", max(ck_att_v, ck_bg_cv))]
                if nmm2 >= 2:
                    w.append(("cs", apev[nmm2 - 2]))
                P.step("t", f, wait=w, inc=("ct", 1))

                def f(s, o=o, qc=qc, bank=bank, nmm2=nmm2):
                    sl = apr[:, (nmm2 % 2) * 512:(nmm2 % 2) * 512 + 512]
                    s.activation(sl, bank[:], AF.Identity,
                                 bias=bo_s[:, o:o + 1], scale=1.0)
                    s.drain()
                    s.dma_start(
                        out=apf_d[o * 128:(o + 1) * 128, qc * 512:qc * 512 + 512],
                        in_=sl).then_inc(sems["qs"], 16)
                    return None
                ws = [("ct", P.cnt("ct"))]
                if nmm2 >= 2:
                    ws.append(apw[nmm2 - 2])
                if o == 7:
                    op = "qs%d" % (1 - P.qpar.get("s", 0))
                    if P.cnt(op) > 0:
                        ws.append((op, P.cnt(op)))
                P.step("s", f, wait=ws, inc=("cs", 1), dma=1)
                apev[nmm2] = P.cnt("cs")
                apw[nmm2] = P.qlast("s")
                nmm2 += 1
            ck_apf_qc[qc] = P.qlast("s")
        ck_apf = P.qlast("s")

        if "apf" in taps:
            def f(g):
                return g.dma_start(out=tap_d["apf"][:, :], in_=apf_d[:, :]
                                   ).then_inc(sems["qg"], 16)
            P.step("g", f, wait=[("qs", ck_apf)], dma=1)

        aptm = dsb("aptm", [128, 3 * E], BF16)
        xr2 = dsb("xr2", [128, 2 * E])
        x1r = dsb("x1r", [128, 3 * E])
        z3r = dsb("z3r", [128, 2 * E], BF16)
        mv2d = dsb("mv2d", [128, 12])
        st12d = dsb("st12d", [128, 12])
        rrd = dsb("rrd", [128, 6])
        mbd = dsb("mbd", [128, 6])
        zmaxc = dsb("zmaxc", [128, NTM])

        x1w = {}
        z3w = {}
        for i in range(NTM):
            sl3 = i % 3
            sl2 = i % 2
            r0 = 2 * sl3
            r1 = 2 * sl3 + 1
            m0 = 4 * sl3
            m2 = 4 * sl3 + 2

            def f(y, i=i, sl2=sl2, sl3=sl3):
                y.dma_start(out=aptm[:, sl3 * E:sl3 * E + E],
                            in_=apf_d[:, i * 128:i * 128 + 128],
                            transpose=True).then_inc(sems["qy"], 16)
                return y.dma_start(out=xr2[:, sl2 * E:sl2 * E + E],
                                   in_=x_d[i * 128:(i + 1) * 128, :]
                                   ).then_inc(sems["qy"], 16)
            w = [ck_apf_qc[i // 4]]
            if i >= 2:
                w.append(z3w[i - 2])
                w.append(x1w[i - 2])
            P.step("y", f, wait=w, dma=2)

            def f(v, sl3=sl3, r0=r0, m0=m0):
                a = aptm[:, sl3 * E:sl3 * E + E]
                v.bn_stats(st12d[:, 0:6], a[:, 0:512])
                v.bn_stats(st12d[:, 6:12], a[:, 512:1024])
                v.drain()
                v.bn_aggr(mv2d[:, m0:m0 + 2], st12d[:])
                v.drain()
                return v.tensor_scalar(rrd[:, r0:r0 + 1],
                                       mv2d[:, m0 + 1:m0 + 2], EPS, None,
                                       AL.add)
            P.step("v", f, wait=[P.qlast("y")], inc=("cv", 1))

            def f(s, r0=r0):
                return s.activation(rrd[:, r0:r0 + 1], rrd[:, r0:r0 + 1],
                                    AF.Sqrt, bias=0.0, scale=1.0)
            P.step("s", f, wait=[("cv", P.cnt("cv"))], inc=("cs", 1))

            def f(v, r0=r0, m0=m0):
                v.reciprocal(rrd[:, r0:r0 + 1], rrd[:, r0:r0 + 1])
                v.drain()
                return v.scalar_tensor_tensor(mbd[:, r0:r0 + 1],
                                              mv2d[:, m0:m0 + 1], -1.0,
                                              rrd[:, r0:r0 + 1], AL.mult,
                                              AL.mult)
            P.step("v", f, wait=[("cs", P.cnt("cs"))], inc=("cv", 1))

            def f(s, sl3=sl3, r0=r0):
                return s.activation(x1r[:, sl3 * E:sl3 * E + E],
                                    aptm[:, sl3 * E:sl3 * E + E], AF.Identity,
                                    bias=mbd[:, r0:r0 + 1],
                                    scale=rrd[:, r0:r0 + 1])
            P.step("s", f, wait=[("cv", P.cnt("cv"))], inc=("cs", 1))

            def f(v, i=i, sl2=sl2, sl3=sl3, r1=r1, m2=m2):
                x1t = x1r[:, sl3 * E:sl3 * E + E]
                v.tensor_tensor(x1t, x1t, xr2[:, sl2 * E:sl2 * E + E], AL.add)
                v.drain()
                v.bn_stats(st12d[:, 0:6], x1t[:, 0:512])
                v.bn_stats(st12d[:, 6:12], x1t[:, 512:1024])
                v.drain()
                v.bn_aggr(mv2d[:, m2:m2 + 2], st12d[:])
                v.drain()
                return v.tensor_scalar(rrd[:, r1:r1 + 1],
                                       mv2d[:, m2 + 1:m2 + 2], EPS, None,
                                       AL.add)
            P.step("v", f, wait=[("cs", P.cnt("cs"))], inc=("cv", 1))

            def f(g, i=i, sl3=sl3):
                return g.dma_start(out=x1_d[i * 128:(i + 1) * 128, :],
                                   in_=x1r[:, sl3 * E:sl3 * E + E]
                                   ).then_inc(sems["qg"], 16)
            P.step("g", f, wait=[("cv", P.cnt("cv"))], dma=1)
            x1w[i] = P.qlast("g")

            def f(s, r1=r1):
                return s.activation(rrd[:, r1:r1 + 1], rrd[:, r1:r1 + 1],
                                    AF.Sqrt, bias=0.0, scale=1.0)
            P.step("s", f, wait=[("cv", P.cnt("cv"))], inc=("cs", 1))

            def f(v, r1=r1, m2=m2):
                v.reciprocal(rrd[:, r1:r1 + 1], rrd[:, r1:r1 + 1])
                v.drain()
                return v.scalar_tensor_tensor(mbd[:, r1:r1 + 1],
                                              mv2d[:, m2:m2 + 1], -1.0,
                                              rrd[:, r1:r1 + 1], AL.mult,
                                              AL.mult)
            P.step("v", f, wait=[("cs", P.cnt("cs"))], inc=("cv", 1))

            def f(s, i=i, sl2=sl2, sl3=sl3, r1=r1):
                s.activation(z3r[:, sl2 * E:sl2 * E + E],
                             x1r[:, sl3 * E:sl3 * E + E], AF.Identity,
                             bias=mbd[:, r1:r1 + 1], scale=rrd[:, r1:r1 + 1])
                s.drain()
                s.dma_start(out=z3_d[i * 128:(i + 1) * 128, :],
                            in_=z3r[:, sl2 * E:sl2 * E + E]
                            ).then_inc(sems["qs"], 16)
                return None
            P.step("s", f, wait=[("cv", P.cnt("cv"))], inc=("cs", 1), dma=1)
            z3w[i] = P.qlast("s")

            def f(v, i=i, sl2=sl2):
                return v.tensor_reduce(zmaxc[:, i:i + 1],
                                       z3r[:, sl2 * E:sl2 * E + E],
                                       AX.X, AL.max,
                                       apply_absolute_value=True)
            P.step("v", f, wait=[("cs", P.cnt("cs"))], inc=("cv", 1))
        ck_z3 = P.qlast("s")
        ck_x1 = P.qlast("g")

        if "x1" in taps:
            def f(g):
                return g.dma_start(out=tap_d["x1"][:, :], in_=x1_d[:, :]
                                   ).then_inc(sems["qg"], 16)
            P.step("g", f, wait=[ck_x1], dma=1)
        if "z3" in taps:
            def f(g):
                return g.dma_start(out=tap_d["z3"][:, :], in_=z3_d[:, :]
                                   ).then_inc(sems["qg"], 16)
            P.step("g", f, wait=[ck_z3], dma=1)

        def allreduce_max(tag, src_col_ap, n, dst_off, wait_cv):
            def f(g, n=n):
                last = None
                for c in range(n):
                    last = g.dma_start(
                        out=mx_i[0:1, 128 * c:128 * c + 128].rearrange(
                            "o (p w) -> (o p) w", w=1),
                        in_=src_col_ap[:, c:c + 1]).then_inc(sems["qg"], 16)
                return last
            P.step("g", f, wait=wait_cv, dma=n)

            def f(g, n=n):
                return g.collective_compute(
                    "AllReduce", AL.max, replica_groups=[list(range(8))],
                    ins=[mx_i[0:1, 0:128 * n]], outs=[mx_o[0:1, 0:128 * n]])
            P.step("g", f, wait=[P.qlast("g")], inc=("cc", 1))

            def f(g, n=n):
                return g.dma_start(out=clrow[0:1, 0:128 * n],
                                   in_=mx_o[0:1, 0:128 * n]
                                   ).then_inc(sems["qg"], 16)
            P.step("g", f, wait=[("cc", P.cnt("cc"))], dma=1)

            def f(v, n=n, dst_off=dst_off):
                for c in range(n):
                    v.tensor_reduce(scr1[0:1, dst_off + c:dst_off + c + 1],
                                    clrow[0:1, 128 * c:128 * c + 128],
                                    AX.X, AL.max)
                    v.drain()
                return v.engine_nop()
            P.step("v", f, wait=[P.qlast("g")], inc=("cv", 1))
            return [("cv", P.cnt("cv"))]

        def f(v):
            v.tensor_reduce(zmaxc[:, 0:1], zmaxc[:, 0:NTM], AX.X, AL.max)
            return v.drain()
        P.step("v", f, wait=[("cv", P.cnt("cv"))], inc=("cv", 1))
        w_ar = allreduce_max("z", zmaxc, 1, 16, [("cv", P.cnt("cv"))])

        def f(v):
            v.tensor_scalar(scr1[0:1, 17:18], scr1[0:1, 16:17], 1.0 / QMAX,
                            1e-12, AL.mult, AL.max)
            v.drain()
            v.reciprocal(scr1[0:1, 18:19], scr1[0:1, 17:18])
            return v.drain()
        P.step("v", f, wait=w_ar, inc=("cv", 1))

        def f(g):
            g.dma_start(out=bc_d[0:1].rearrange("(o n) -> o n", o=1),
                        in_=scr1[0:1, 18:19]).then_inc(sems["qg"], 16)
            return None
        P.step("g", f, wait=[("cv", P.cnt("cv"))], dma=1)

        def f(g):
            return g.dma_start(out=sca[:, 0:1],
                               in_=bass.AP(bc_t, 0, [[0, 128], [1, 1]])
                               ).then_inc(sems["qg"], 16)
        P.step("g", f, wait=[P.qlast("g")], dma=1)

        def f(g):
            return g.engine_nop()
        P.step("g", f, wait=[P.qlast("g")], inc=("cg", 1))
        ck_rz = P.cnt("cg")

        pD.close()
        pC.close()
        attsc.close()
        phase_barrier()

        pE1 = contextlib.ExitStack()

        def e1sb(name, shape, dt=F32):
            return pE1.enter_context(nc.sbuf_tensor(name, shape, dt))

        zf2 = e1sb("zf2", [128, 8 * TOK], BF16)
        zsc = e1sb("zsc", [128, TOK])
        wfu = e1sb("wfu", [128, 8 * I3], BF16)
        wfg = e1sb("wfg", [128, 8 * I3], BF16)
        ffs = e1sb("ffs", [128, 4 * 512], BF16)
        pf = [pE1.enter_context(nc.psum_tensor("pf%d" % i, [128, 512], F32))
              for i in range(2)]

        def f(y):
            last = None
            for c in range(8):
                last = y.dma_start(out=zf2[:, c * TOK:(c + 1) * TOK],
                                   in_=z3_d[:, c * 128:(c + 1) * 128],
                                   transpose=True).then_inc(sems["qy"], 16)
            return last
        P.step("y", f, wait=[ck_z3, ck_bg_qy], dma=8)
        ck_zf2 = P.qlast("y")

        for c in range(8):
            def f(v, c=c):
                v.tensor_scalar(zsc[:], zf2[:, c * TOK:(c + 1) * TOK],
                                sca[:, 0:1], MAGIC, AL.mult, AL.add)
                v.drain()
                return v.tensor_scalar(zf2[:, c * TOK:(c + 1) * TOK], zsc[:],
                                       MAGIC, 1.0, AL.subtract, AL.mult)
            P.step("v", f, wait=[ck_zf2, ("cg", ck_rz),
                                 ("cv", P.cnt("cv"))], inc=("cv", 1))
        ck_zq = P.cnt("cv")

        def f(y):
            last = None
            for fc in range(8):
                last = y.dma_start(out=wfu[:, fc * I3:(fc + 1) * I3],
                                   in_=wq_d["up"][:, fc * 128:(fc + 1) * 128],
                                   transpose=True).then_inc(sems["qy"], 16)
            for fc in range(8):
                last = y.dma_start(out=wfg[:, fc * I3:(fc + 1) * I3],
                                   in_=wq_d["gate"][:, fc * 128:(fc + 1) * 128],
                                   transpose=True).then_inc(sems["qy"], 16)
            return last
        P.step("y", f, wait=[ck_zf2], dma=16)
        ck_wf = P.qlast("y")

        sp_done = {}
        sp_done_q = {}
        nmf = 0
        for kind, wf, dst in (("up", wfu, u_d), ("gate", wfg, g_d)):
            for ic in range(NI):
                for qc in range(NQ5):
                    bank = pf[nmf % 2]

                    def f(t, wf=wf, ic=ic, qc=qc, bank=bank):
                        last = None
                        for fc in range(8):
                            last = t.matmul(
                                bank[:],
                                wf[:, fc * I3 + ic * 128:fc * I3 + ic * 128 + 128],
                                zf2[:, fc * TOK + qc * 512:fc * TOK + qc * 512 + 512],
                                start=(fc == 0), stop=(fc == 7))
                        return last
                    w = [("cv", ck_zq), ck_wf]
                    if nmf >= 2:
                        w.append(("cv", sp_done[nmf - 2]))
                    P.step("t", f, wait=w, inc=("ct", 1))

                    mcol = nmf

                    def f(v, bank=bank, nmf=nmf, mcol=mcol):
                        sl = ffs[:, (nmf % 4) * 512:(nmf % 4) * 512 + 512]
                        v.tensor_reduce(ffmax[:, mcol:mcol + 1], bank[:],
                                        AX.X, AL.max,
                                        apply_absolute_value=True)
                        return v.tensor_copy(sl, bank[:])
                    wv = [("ct", P.cnt("ct"))]
                    if nmf >= 4:
                        wv.append(sp_done_q[nmf - 4])
                    P.step("v", f, wait=wv, inc=("cv", 1))
                    sp_done[nmf] = P.cnt("cv")

                    def f(g, nmf=nmf, dst=dst, ic=ic, qc=qc):
                        sl = ffs[:, (nmf % 4) * 512:(nmf % 4) * 512 + 512]
                        return g.dma_start(
                            out=dst[ic * 128:(ic + 1) * 128,
                                    qc * 512:qc * 512 + 512],
                            in_=sl).then_inc(sems["qg"], 16)
                    P.step("g", f, wait=[("cv", P.cnt("cv"))], dma=1)
                    sp_done_q[nmf] = P.qlast("g")
                    nmf += 1
        ck_s1_v = P.cnt("cv")
        ck_s1_q = P.qlast("g")

        if "uu" in taps:
            def f(g):
                return g.dma_start(out=tap_d["uu"][:, :], in_=u_d[:, :]
                                   ).then_inc(sems["qg"], 16)
            P.step("g", f, wait=[ck_s1_q], dma=1)
        if "gg" in taps:
            def f(g):
                return g.dma_start(out=tap_d["gg"][:, :], in_=g_d[:, :]
                                   ).then_inc(sems["qg"], 16)
            P.step("g", f, wait=[ck_s1_q], dma=1)

        NU = NI * NQ5

        def f(v):
            v.tensor_reduce(ffmax[:, 0:1], ffmax[:, 0:NU], AX.X, AL.max)
            v.drain()
            v.tensor_reduce(ffmax[:, 1:2], ffmax[:, NU:2 * NU], AX.X, AL.max)
            return v.drain()
        P.step("v", f, wait=[("cv", ck_s1_v)], inc=("cv", 1))
        w_ar = allreduce_max("ug", ffmax, 2, 19, [("cv", P.cnt("cv"))])

        def f(v):
            v.tensor_tensor(scr1[0:1, 21:22], scr1[0:1, 19:20],
                            scr1[0:1, 17:18], AL.mult)
            v.drain()
            v.tensor_scalar(scr1[0:1, 21:22], scr1[0:1, 21:22], 1.0 / QMAX,
                            1e-12, AL.mult, AL.max)
            v.drain()
            v.reciprocal(scr1[0:1, 22:23], scr1[0:1, 21:22])
            v.drain()
            v.tensor_tensor(scr1[0:1, 22:23], scr1[0:1, 22:23],
                            scr1[0:1, 17:18], AL.mult)
            v.drain()
            v.tensor_tensor(scr1[0:1, 23:24], scr1[0:1, 20:21],
                            scr1[0:1, 17:18], AL.mult)
            v.drain()
            v.tensor_scalar(scr1[0:1, 23:24], scr1[0:1, 23:24], 1.0 / QMAX,
                            1e-12, AL.mult, AL.max)
            v.drain()
            v.reciprocal(scr1[0:1, 30:31], scr1[0:1, 23:24])
            v.drain()
            v.tensor_tensor(scr1[0:1, 30:31], scr1[0:1, 30:31],
                            scr1[0:1, 17:18], AL.mult)
            return v.drain()
        P.step("v", f, wait=w_ar, inc=("cv", 1))

        def f(g):
            g.dma_start(out=bc_d[0:1].rearrange("(o n) -> o n", o=1),
                        in_=scr1[0:1, 22:23]).then_inc(sems["qg"], 16)
            g.dma_start(out=bc_d[1:2].rearrange("(o n) -> o n", o=1),
                        in_=scr1[0:1, 30:31]).then_inc(sems["qg"], 16)
            g.dma_start(out=bc_d[2:3].rearrange("(o n) -> o n", o=1),
                        in_=scr1[0:1, 23:24]).then_inc(sems["qg"], 16)
            return None
        P.step("g", f, wait=[("cv", P.cnt("cv"))], dma=3)

        def f(g):
            g.dma_start(out=sca[:, 2:3],
                        in_=bass.AP(bc_t, 0, [[0, 128], [1, 1]])
                        ).then_inc(sems["qg"], 16)
            g.dma_start(out=sca[:, 3:4],
                        in_=bass.AP(bc_t, 1, [[0, 128], [1, 1]])
                        ).then_inc(sems["qg"], 16)
            g.dma_start(out=sca[:, 4:5],
                        in_=bass.AP(bc_t, 2, [[0, 128], [1, 1]])
                        ).then_inc(sems["qg"], 16)
            return None
        P.step("g", f, wait=[P.qlast("g")], dma=3)

        def f(g):
            return g.engine_nop()
        P.step("g", f, wait=[P.qlast("g")], inc=("cg", 1))
        ck_kug = P.cnt("cg")

        pE1.close()
        phase_barrier()

        pE2 = contextlib.ExitStack()

        def e2sb(name, shape, dt=F32):
            return pE2.enter_context(nc.sbuf_tensor(name, shape, dt))

        hpt = e2sb("hpt", [128, NI * TOK], BF16)
        wfd = e2sb("wfd", [128, NI * E], BF16)
        yfr = e2sb("yfr", [128, 2 * 512], BF16)
        pE2b = contextlib.ExitStack()

        def e2bsb(name, shape, dt=F32):
            return pE2b.enter_context(nc.sbuf_tensor(name, shape, dt))

        gur = e2bsb("gur", [128, 2 * TOK], BF16)
        fsc = e2bsb("fsc", [128, TOK])
        fsc2 = e2bsb("fsc2", [128, TOK])
        fsb = e2bsb("fsb", [128, 2 * TOK], BF16)

        def f(y):
            last = None
            for ic in range(NI):
                last = y.dma_start(out=wfd[:, ic * E:(ic + 1) * E],
                                   in_=wq_d["down"][:, ic * 128:(ic + 1) * 128],
                                   transpose=True).then_inc(sems["qy"], 16)
            return last
        P.step("y", f, wait=[ck_wf], dma=NI)
        ck_wfd = P.qlast("y")

        h_done = {}
        sig_cs = {}
        gp2_cv = {"g": 0, "s": 0}
        pend_sh = None

        def emit_sh(ic):
            sl2 = ic % 2

            def f(v, ic=ic, sl2=sl2):
                return v.tensor_scalar(fsb[:, sl2 * TOK:sl2 * TOK + TOK],
                                       fsc2[:], MAGIC, 65535.0, AL.subtract,
                                       AL.min)
            P.step("v", f, wait=[("cs", sig_cs[ic])], inc=("cv", 1))
            gp2_cv["s"] = P.cnt("cv")

            def f(v, ic=ic, sl2=sl2):
                v.tensor_tensor(hpt[:, ic * TOK:(ic + 1) * TOK],
                                gur[:, sl2 * TOK:sl2 * TOK + TOK],
                                fsb[:, sl2 * TOK:sl2 * TOK + TOK], AL.mult)
                v.drain()
                return v.tensor_reduce(ffmax[:, ic:ic + 1],
                                       hpt[:, ic * TOK:(ic + 1) * TOK], AX.X,
                                       AL.max, apply_absolute_value=True)
            P.step("v", f, wait=[("cv", P.cnt("cv"))], inc=("cv", 1))
            h_done[ic] = P.cnt("cv")

        for ic in range(NI):
            sl2 = ic % 2

            def f(y, ic=ic, sl2=sl2):
                return y.dma_start(out=gur[:, sl2 * TOK:sl2 * TOK + TOK],
                                   in_=g_d[ic * 128:(ic + 1) * 128, :]
                                   ).then_inc(sems["qy"], 16)
            w = [ck_s1_q]
            if ic >= 2:
                w.append(("cv", h_done[ic - 2]))
            P.step("y", f, wait=w, dma=1)

            def f(s, sl2=sl2):
                return s.activation(fsc[:], gur[:, sl2 * TOK:sl2 * TOK + TOK],
                                    AF.Identity, bias=mgcb[:],
                                    scale=sca[:, 3:4])
            ws = [P.qlast("y"), ("cg", ck_kug), ("cv", gp2_cv["g"])]
            P.step("s", f, wait=ws, inc=("cs", 1))

            def f(v, sl2=sl2):
                return v.tensor_scalar(gur[:, sl2 * TOK:sl2 * TOK + TOK],
                                       fsc[:], MAGIC, 1.0, AL.subtract,
                                       AL.mult)
            P.step("v", f, wait=[("cs", P.cnt("cs"))], inc=("cv", 1))
            gp2_cv["g"] = P.cnt("cv")

            if pend_sh is not None:
                emit_sh(pend_sh)
                pend_sh = None

            def f(s, sl2=sl2):
                s.activation(fsb[:, sl2 * TOK:sl2 * TOK + TOK],
                             gur[:, sl2 * TOK:sl2 * TOK + TOK],
                             AF.Sigmoid, bias=0.0, scale=sca[:, 4:5])
                s.drain()
                return s.activation(fsc2[:],
                                    fsb[:, sl2 * TOK:sl2 * TOK + TOK],
                                    AF.Identity, bias=mgcb[:], scale=65536.0)
            ws = [("cv", gp2_cv["g"]), ("cv", gp2_cv["s"])]
            if ic >= 2:
                ws.append(("cv", h_done[ic - 2]))
            P.step("s", f, wait=ws, inc=("cs", 1))
            sig_cs[ic] = P.cnt("cs")
            pend_sh = ic
        emit_sh(pend_sh)

        def f(v):
            v.tensor_reduce(ffmax[:, 0:1], ffmax[:, 0:NI], AX.X, AL.max)
            return v.drain()
        P.step("v", f, wait=[("cv", P.cnt("cv"))], inc=("cv", 1))
        w_ar = allreduce_max("h", ffmax, 1, 19, [("cv", P.cnt("cv"))])

        def f(v):
            v.tensor_scalar(scr1[0:1, 20:21], scr1[0:1, 23:24], 1.0 / 65536.0,
                            None, AL.mult)
            v.drain()
            v.tensor_tensor(scr1[0:1, 21:22], scr1[0:1, 20:21],
                            scr1[0:1, 19:20], AL.mult)
            v.drain()
            v.tensor_scalar(scr1[0:1, 21:22], scr1[0:1, 21:22], 1.0 / QMAX,
                            1e-12, AL.mult, AL.max)
            v.drain()
            v.reciprocal(scr1[0:1, 22:23], scr1[0:1, 21:22])
            v.drain()
            v.tensor_tensor(scr1[0:1, 22:23], scr1[0:1, 22:23],
                            scr1[0:1, 20:21], AL.mult)
            return v.drain()
        P.step("v", f, wait=w_ar, inc=("cv", 1))

        def f(g):
            g.dma_start(out=bc_d[0:1].rearrange("(o n) -> o n", o=1),
                        in_=scr1[0:1, 22:23]).then_inc(sems["qg"], 16)
            return None
        P.step("g", f, wait=[("cv", P.cnt("cv"))], dma=1)

        def f(g):
            return g.dma_start(out=sca[:, 5:6],
                               in_=bass.AP(bc_t, 0, [[0, 128], [1, 1]])
                               ).then_inc(sems["qg"], 16)
        P.step("g", f, wait=[P.qlast("g")], dma=1)

        def f(g):
            return g.engine_nop()
        P.step("g", f, wait=[P.qlast("g")], inc=("cg", 1))
        ck_kh = P.cnt("cg")
        ck_stage2_v = P.cnt("cv")

        p_done = {}
        for ic in range(NI):
            sl2 = ic % 2

            def f(y, ic=ic, sl2=sl2):
                return y.dma_start(out=gur[:, sl2 * TOK:sl2 * TOK + TOK],
                                   in_=u_d[ic * 128:(ic + 1) * 128, :]
                                   ).then_inc(sems["qy"], 16)
            w = [("cv", ck_stage2_v)]
            if ic >= 2:
                w.append(("cv", p_done[ic - 2]))
            P.step("y", f, wait=w, dma=1)

            def f(s, sl2=sl2):
                return s.activation(fsc[:], gur[:, sl2 * TOK:sl2 * TOK + TOK],
                                    AF.Identity, bias=mgcb[:],
                                    scale=sca[:, 2:3])
            P.step("s", f, wait=[P.qlast("y"), ("cg", ck_kug),
                                 ("cv", gp2_cv["g"])], inc=("cs", 1))

            def f(v, sl2=sl2):
                return v.tensor_scalar(fsb[:, sl2 * TOK:sl2 * TOK + TOK],
                                       fsc[:], MAGIC, 1.0, AL.subtract,
                                       AL.mult)
            P.step("v", f, wait=[("cs", P.cnt("cs"))], inc=("cv", 1))
            gp2_cv["g"] = P.cnt("cv")

            def f(s, ic=ic):
                return s.activation(fsc2[:], hpt[:, ic * TOK:(ic + 1) * TOK],
                                    AF.Identity, bias=mgcb[:],
                                    scale=sca[:, 5:6])
            P.step("s", f, wait=[("cv", gp2_cv["s"]), ("cg", ck_kh),
                                 ("cv", P.cnt("cv"))], inc=("cs", 1))

            def f(v, ic=ic):
                return v.tensor_scalar(hpt[:, ic * TOK:(ic + 1) * TOK],
                                       fsc2[:], MAGIC, 1.0, AL.subtract,
                                       AL.mult)
            P.step("v", f, wait=[("cs", P.cnt("cs"))], inc=("cv", 1))
            gp2_cv["s"] = P.cnt("cv")

            def f(v, ic=ic, sl2=sl2):
                v.tensor_tensor(hpt[:, ic * TOK:(ic + 1) * TOK],
                                hpt[:, ic * TOK:(ic + 1) * TOK],
                                fsb[:, sl2 * TOK:sl2 * TOK + TOK],
                                AL.mult)
                v.drain()
                return v.tensor_reduce(ffmax[:, ic:ic + 1],
                                       hpt[:, ic * TOK:(ic + 1) * TOK], AX.X,
                                       AL.max, apply_absolute_value=True)
            P.step("v", f, wait=[("cv", P.cnt("cv"))], inc=("cv", 1))
            p_done[ic] = P.cnt("cv")

        def f(v):
            v.tensor_reduce(ffmax[:, 0:1], ffmax[:, 0:NI], AX.X, AL.max)
            return v.drain()
        P.step("v", f, wait=[("cv", P.cnt("cv"))], inc=("cv", 1))
        w_ar = allreduce_max("p", ffmax, 1, 19, [("cv", P.cnt("cv"))])

        def f(v):
            v.tensor_tensor(scr1[0:1, 25:26], scr1[0:1, 21:22],
                            scr1[0:1, 17:18], AL.mult)
            v.drain()
            v.reciprocal(scr1[0:1, 26:27], sca[0:1, 2:3])
            v.drain()
            v.tensor_tensor(scr1[0:1, 25:26], scr1[0:1, 25:26],
                            scr1[0:1, 26:27], AL.mult)
            v.drain()
            v.tensor_tensor(scr1[0:1, 27:28], scr1[0:1, 25:26],
                            scr1[0:1, 19:20], AL.mult)
            v.drain()
            v.tensor_scalar(scr1[0:1, 27:28], scr1[0:1, 27:28], 1.0 / QMAX,
                            1e-12, AL.mult, AL.max)
            v.drain()
            v.reciprocal(scr1[0:1, 28:29], scr1[0:1, 27:28])
            v.drain()
            v.tensor_tensor(scr1[0:1, 28:29], scr1[0:1, 28:29],
                            scr1[0:1, 25:26], AL.mult)
            v.drain()
            v.tensor_tensor(scr1[0:1, 29:30], scr1[0:1, 27:28],
                            scr1[0:1, 27:28], AL.mult)
            return v.drain()
        P.step("v", f, wait=w_ar, inc=("cv", 1))

        def f(g):
            g.dma_start(out=bc_d[0:1].rearrange("(o n) -> o n", o=1),
                        in_=scr1[0:1, 28:29]).then_inc(sems["qg"], 16)
            g.dma_start(out=bc_d[1:2].rearrange("(o n) -> o n", o=1),
                        in_=scr1[0:1, 27:28]).then_inc(sems["qg"], 16)
            g.dma_start(out=bc_d[2:3].rearrange("(o n) -> o n", o=1),
                        in_=scr1[0:1, 29:30]).then_inc(sems["qg"], 16)
            return None
        P.step("g", f, wait=[("cv", P.cnt("cv"))], dma=3)

        def f(g):
            g.dma_start(out=sca[:, 6:7],
                        in_=bass.AP(bc_t, 0, [[0, 128], [1, 1]])
                        ).then_inc(sems["qg"], 16)
            g.dma_start(out=sca[:, 7:8],
                        in_=bass.AP(bc_t, 1, [[0, 128], [1, 1]])
                        ).then_inc(sems["qg"], 16)
            g.dma_start(out=sca[:, 8:9],
                        in_=bass.AP(bc_t, 2, [[0, 128], [1, 1]])
                        ).then_inc(sems["qg"], 16)
            return None
        P.step("g", f, wait=[P.qlast("g")], dma=3)

        def f(g):
            return g.engine_nop()
        P.step("g", f, wait=[P.qlast("g")], inc=("cg", 1))
        ck_kp = P.cnt("cg")

        if "sca" in taps:
            def f(g):
                return g.dma_start(out=tap_d["sca"][:, :], in_=sca[:]
                                   ).then_inc(sems["qg"], 16)
            P.step("g", f, wait=[("cg", ck_kp)], dma=1)

        o_done = {}
        for ic in range(NI):
            def f(v, ic=ic):
                v.tensor_scalar(fsc[:], hpt[:, ic * TOK:(ic + 1) * TOK],
                                sca[:, 6:7], MAGIC, AL.mult, AL.add)
                v.drain()
                return v.tensor_scalar(hpt[:, ic * TOK:(ic + 1) * TOK],
                                       fsc[:], MAGIC, 1.0, AL.subtract, AL.mult)
            P.step("v", f, wait=[("cg", ck_kp), ("cv", P.cnt("cv"))],
                   inc=("cv", 1))
            o_done[ic] = P.cnt("cv")
        ck_oall = P.cnt("cv")

        if "hh" in taps:
            def f(g):
                return g.dma_start(out=tap_d["hh"][:, :], in_=hpt[:]
                                   ).then_inc(sems["qg"], 16)
            P.step("g", f, wait=[("cv", ck_oall)], dma=1)

        pE2b.close()
        phase_barrier()

        pF = contextlib.ExitStack()

        def fsb2(name, shape, dt=F32):
            return pF.enter_context(nc.sbuf_tensor(name, shape, dt))

        ydone = {}
        ydw = {}
        ck_yf_qc = {}
        nmd = 0
        pfd = [pE2.enter_context(nc.psum_tensor("pfd%d" % i, [128, 512], F32))
               for i in range(2)]
        for qc in range(NQ5):
            for o in range(8):
                bank = pfd[nmd % 2]

                def f(t, o=o, qc=qc, bank=bank):
                    last = None
                    for ic in range(NI):
                        last = t.matmul(
                            bank[:],
                            wfd[:, ic * E + o * 128:ic * E + o * 128 + 128],
                            hpt[:, ic * TOK + qc * 512:ic * TOK + qc * 512 + 512],
                            start=(ic == 0), stop=(ic == NI - 1))
                    return last
                w = [("cv", ck_oall), ck_wfd]
                if nmd >= 2:
                    w.append(("cs", ydone[nmd - 2]))
                P.step("t", f, wait=w, inc=("ct", 1))

                def f(s, o=o, qc=qc, bank=bank, nmd=nmd):
                    sl = yfr[:, (nmd % 2) * 512:(nmd % 2) * 512 + 512]
                    s.activation(sl, bank[:], AF.Identity, bias=0.0, scale=1.0)
                    s.drain()
                    s.dma_start(
                        out=yf_d[o * 128:(o + 1) * 128, qc * 512:qc * 512 + 512],
                        in_=sl).then_inc(sems["qs"], 16)
                    return None
                ws = [("ct", P.cnt("ct"))]
                if nmd >= 2:
                    ws.append(ydw[nmd - 2])
                if o == 7:
                    op = "qs%d" % (1 - P.qpar.get("s", 0))
                    if P.cnt(op) > 0:
                        ws.append((op, P.cnt(op)))
                P.step("s", f, wait=ws, inc=("cs", 1), dma=1)
                ydone[nmd] = P.cnt("cs")
                ydw[nmd] = P.qlast("s")
                nmd += 1
            ck_yf_qc[qc] = P.qlast("s")
        ck_yf = P.qlast("s")

        if "yy" in taps:
            def f(g):
                return g.dma_start(out=tap_d["yy"][:, :], in_=yf_d[:, :]
                                   ).then_inc(sems["qg"], 16)
            P.step("g", f, wait=[ck_yf], dma=1)

        ytm = fsb2("ytm", [128, 2 * E], BF16)
        x1r2 = fsb2("x1r2", [128, 2 * E])
        otr = fsb2("otr", [128, 2 * E])
        mv2f = fsb2("mv2f", [128, 2])
        st12f = fsb2("st12f", [128, 12])
        rrf = fsb2("rrf", [128, 1])
        mbf = fsb2("mbf", [128, 1])

        ow = {}
        for i in range(NTM):
            sl2 = i % 2

            def f(y, i=i, sl2=sl2):
                y.dma_start(out=ytm[:, sl2 * E:sl2 * E + E],
                            in_=yf_d[:, i * 128:i * 128 + 128],
                            transpose=True).then_inc(sems["qy"], 16)
                return y.dma_start(out=x1r2[:, sl2 * E:sl2 * E + E],
                                   in_=x1_d[i * 128:(i + 1) * 128, :]
                                   ).then_inc(sems["qy"], 16)
            w = [ck_yf_qc[i // 4], ck_x1]
            if i >= 2:
                w.append(ow[i - 2])
            P.step("y", f, wait=w, dma=2)

            def f(v, sl2=sl2):
                a = ytm[:, sl2 * E:sl2 * E + E]
                v.bn_stats(st12f[:, 0:6], a[:, 0:512])
                v.bn_stats(st12f[:, 6:12], a[:, 512:1024])
                v.drain()
                v.bn_aggr(mv2f[:], st12f[:])
                v.drain()
                return v.tensor_scalar(rrf[:], mv2f[:, 1:2], sca[:, 8:9],
                                       EPS, AL.mult, AL.add)
            P.step("v", f, wait=[P.qlast("y")], inc=("cv", 1))

            def f(s):
                return s.activation(rrf[:], rrf[:], AF.Sqrt, bias=0.0,
                                    scale=1.0)
            P.step("s", f, wait=[("cv", P.cnt("cv"))], inc=("cs", 1))

            def f(v):
                v.reciprocal(rrf[:], rrf[:])
                v.drain()
                v.tensor_tensor(rrf[:], rrf[:], sca[:, 7:8], AL.mult)
                v.drain()
                return v.scalar_tensor_tensor(mbf[:], mv2f[:, 0:1], -1.0,
                                              rrf[:], AL.mult, AL.mult)
            P.step("v", f, wait=[("cs", P.cnt("cs"))], inc=("cv", 1))

            def f(s, sl2=sl2):
                return s.activation(otr[:, sl2 * E:sl2 * E + E],
                                    ytm[:, sl2 * E:sl2 * E + E], AF.Identity,
                                    bias=mbf[:], scale=rrf[:])
            P.step("s", f, wait=[("cv", P.cnt("cv"))], inc=("cs", 1))

            def f(v, i=i, sl2=sl2):
                ot = otr[:, sl2 * E:sl2 * E + E]
                return v.tensor_tensor(ot, ot, x1r2[:, sl2 * E:sl2 * E + E],
                                       AL.add)
            P.step("v", f, wait=[("cs", P.cnt("cs"))], inc=("cv", 1))

            def f(g, i=i, sl2=sl2):
                return g.dma_start(out=out_d[i * 128:(i + 1) * 128, :],
                                   in_=otr[:, sl2 * E:sl2 * E + E]
                                   ).then_inc(sems["qg"], 16)
            P.step("g", f, wait=[("cv", P.cnt("cv"))], dma=1)
            ow[i] = P.qlast("g")

        def f(g):
            return g.engine_nop()
        P.step("g", f, wait=[("qg0", P.cnt("qg0")), ("qg1", P.cnt("qg1"))],
               inc=("cg", 1))

        def f(y):
            return None
        P.step("y", f, wait=[("qy0", P.cnt("qy0")), ("qy1", P.cnt("qy1"))])

        def f(s):
            return None
        P.step("s", f, wait=[("qs0", P.cnt("qs0")), ("qs1", P.cnt("qs1"))])

        pF.close()
        pE2.close()

        with nc.Block() as blk:
            for ename in ("y", "v", "s", "t", "g"):
                def make(ename):
                    def emit(eng):
                        lastw = {}
                        for (e, fn, w, inc, semn) in P.steps:
                            if e != ename:
                                continue
                            for (nm, val) in w:
                                if lastw.get(nm, -1) < val:
                                    eng.wait_ge(sems[nm], val)
                                    lastw[nm] = val
                            if semn is not None:
                                sems["q" + ename] = sems[semn]
                            r = fn(eng)
                            if inc is not None:
                                if r is None:
                                    r = eng.nop()
                                r.then_inc(sems[inc[0]], inc[1])
                    return emit
                getattr(blk, ENGMAP[ename])(make(ename))

    return nc


def _prep_inputs(inputs, TOK=2048):
    Wq = np.asarray(inputs["L0_attn_Wq"], dtype=np.float32)
    Wk = np.asarray(inputs["L0_attn_Wk"], dtype=np.float32)
    Wv = np.asarray(inputs["L0_attn_Wv"], dtype=np.float32)
    Wo = np.asarray(inputs["L0_attn_Wo"], dtype=np.float32)
    bo = np.asarray(inputs["L0_attn_bo"], dtype=np.float32)
    wup = np.asarray(inputs["L0_ffn_w_up"], dtype=np.float32)
    wgate = np.asarray(inputs["L0_ffn_w_gate"], dtype=np.float32)
    wdown = np.asarray(inputs["L0_ffn_w_down"], dtype=np.float32)
    x = np.asarray(inputs["L0_attn_x"], dtype=np.float32)

    rowperm = np.array([PAIRS[p][jj] * D + d for p in range(8)
                        for jj in range(2) for d in range(D)], dtype=np.int64)
    wqt = np.ascontiguousarray(Wq[rowperm, :].T)
    wkt = np.ascontiguousarray(Wk.T)
    wvt = np.ascontiguousarray(Wv.T)
    wot = np.ascontiguousarray(Wo[:, rowperm].T)
    bo2 = np.ascontiguousarray(bo.reshape(8, 128).T)

    base = {"wqt": wqt, "wkt": wkt, "wvt": wvt, "wot": wot, "bo": bo2,
            "wup": np.ascontiguousarray(wup),
            "wgate": np.ascontiguousarray(wgate),
            "wdown": np.ascontiguousarray(wdown)}
    in_maps = []
    B = x.shape[0]
    S = x.shape[1]
    for c in range(2 * B):
        b, h = c // 2, c % 2
        if h == 0:
            xs = x[b, 0:TOK, :]
        else:
            xs = x[b, S - 1:S - TOK - 1:-1, :]
        m = dict(base)
        m["x"] = np.ascontiguousarray(xs)
        m["poff"] = np.array([[(1 - h) * 2 * NKV * D * TOK]], dtype=np.int32)
        in_maps.append(m)
    return in_maps


def kernel(L0_attn_x, L0_attn_Wq, L0_attn_Wk, L0_attn_Wv, L0_attn_Wo,
           L0_attn_bo, L0_ffn_w_gate, L0_ffn_w_up, L0_ffn_w_down):
    x = np.asarray(L0_attn_x, dtype=np.float32)
    B, S, _ = x.shape
    TOK = S // 2
    inputs = {"L0_attn_x": x, "L0_attn_Wq": L0_attn_Wq,
              "L0_attn_Wk": L0_attn_Wk, "L0_attn_Wv": L0_attn_Wv,
              "L0_attn_Wo": L0_attn_Wo, "L0_attn_bo": L0_attn_bo,
              "L0_ffn_w_gate": L0_ffn_w_gate, "L0_ffn_w_up": L0_ffn_w_up,
              "L0_ffn_w_down": L0_ffn_w_down}
    in_maps = _prep_inputs(inputs, TOK=TOK)
    nc = build_nc(TOK=TOK)
    res = run_bass_kernel_spmd(nc, in_maps, core_ids=list(range(8)))
    out = np.empty((B, S, E), dtype=np.float32)
    for c in range(2 * B):
        b, h = c // 2, c % 2
        o = np.asarray(res.results[c]["out"])
        if h == 0:
            out[b, 0:TOK, :] = o
        else:
            out[b, TOK:S, :] = o[::-1, :]
    return out


# version 20 (final)
# speedup vs baseline: 1.0293x; 1.0293x over previous
import contextlib

import numpy as np
import ml_dtypes

import concourse.bass as bass
import concourse.mybir as mybir
from concourse.bass_utils import run_bass_kernel_spmd


F32 = mybir.dt.float32
BF16 = mybir.dt.bfloat16
BF = ml_dtypes.bfloat16
AF = mybir.ActivationFunctionType
AL = mybir.AluOpType
AX = mybir.AxisListType

MAGIC = float(np.float32(1.5 * 2 ** 23))
QMAX = 32767.0
E = 1024
NH = 16
NKV = 8
D = 64
I3 = 3072
NI = I3 // 128
EPS = 1e-5

PAIRS = [(0, 2), (1, 3), (4, 6), (5, 7), (8, 10), (9, 11), (12, 14), (13, 15)]
ENGMAP = {"y": "sync", "v": "vector", "s": "scalar", "t": "tensor", "g": "gpsimd"}


def _slope(h):
    return 2.0 ** (-(h + 1))


class Prog:

    def __init__(self):
        self.steps = []
        self.n = {}
        self.qpar = {}
        self.qlastd = {}

    def cnt(self, sem):
        return self.n.get(sem, 0)

    def bump(self, sem, amt):
        self.n[sem] = self.n.get(sem, 0) + amt
        return self.n[sem]

    def qlast(self, eng):
        return self.qlastd.get(eng, ("q%s0" % eng, 0))

    def step(self, eng, fn, wait=(), inc=None, dma=0):
        w = [(s, v) for s, v in wait if v > 0]
        semn = None
        if dma:
            par = self.qpar.get(eng, 0)
            semn = "q%s%d" % (eng, par)
            pre = self.cnt(semn)
            if pre > 0:
                w.append((semn, pre))
            self.bump(semn, 16 * dma)
            self.qpar[eng] = 1 - par
            self.qlastd[eng] = (semn, self.cnt(semn))
        if inc is not None:
            self.bump(inc[0], inc[1])
        self.steps.append((eng, fn, w, inc, semn))


def build_nc(TOK=2048, taps=()):
    S = 2 * TOK
    NKC = S // 128
    NTM = TOK // 128
    NQ5 = TOK // 512
    taps = tuple(taps)

    def qblocks(p):
        if p == 0 or TOK < 1024:
            return [(i * 512, 512) for i in range(TOK // 512)]
        return [(i * 1024, 1024) for i in range(TOK // 1024)]

    nc = bass.Bass("TRN2", target_bir_lowering=False)

    x_d = nc.dram_tensor("x", [TOK, E], F32, kind="ExternalInput").ap()
    wqt_d = nc.dram_tensor("wqt", [E, E], F32, kind="ExternalInput").ap()
    wkt_d = nc.dram_tensor("wkt", [E, NKV * D], F32, kind="ExternalInput").ap()
    wvt_d = nc.dram_tensor("wvt", [E, NKV * D], F32, kind="ExternalInput").ap()
    wot_d = nc.dram_tensor("wot", [E, E], F32, kind="ExternalInput").ap()
    bo_d = nc.dram_tensor("bo", [128, 8], F32, kind="ExternalInput").ap()
    wup_d = nc.dram_tensor("wup", [I3, E], F32, kind="ExternalInput").ap()
    wgate_d = nc.dram_tensor("wgate", [I3, E], F32, kind="ExternalInput").ap()
    wdown_d = nc.dram_tensor("wdown", [E, I3], F32, kind="ExternalInput").ap()
    out_d = nc.dram_tensor("out", [TOK, E], F32, kind="ExternalOutput").ap()
    poff_d = nc.dram_tensor("poff", [1, 1], mybir.dt.int32,
                            kind="ExternalInput").ap()

    TAPSPEC = {
        "z1": ([TOK, E], BF16), "z3": ([TOK, E], BF16),
        "qt": ([128, 8 * TOK], BF16), "kt": ([128, 4 * S], BF16),
        "vt": ([128, (S // 128) * 512], BF16), "att": ([128, 8 * TOK], BF16),
        "apf": ([E, TOK], BF16), "x1": ([TOK, E], F32),
        "uu": ([I3, TOK], BF16), "gg": ([I3, TOK], BF16),
        "hh": ([128, NI * TOK], BF16), "sca": ([128, 16], F32),
        "bb": ([128, NH], F32), "yy": ([E, TOK], BF16),
        "bias": ([128, 4 * (S // 128)], F32), "wq": ([I3, E], BF16),
    }
    tap_d = {t: nc.dram_tensor("tap_" + t, TAPSPEC[t][0], TAPSPEC[t][1],
                               kind="ExternalOutput").ap() for t in taps}

    z1_d = nc.dram_tensor("z1s", [TOK, E], BF16).ap()
    z3_d = nc.dram_tensor("z3s", [TOK, E], BF16).ap()
    apf_d = nc.dram_tensor("apfs", [E, TOK], BF16).ap()
    yf_d = nc.dram_tensor("yfs", [E, TOK], BF16).ap()
    x1_d = nc.dram_tensor("x1s", [TOK, E], F32).ap()
    u_d = nc.dram_tensor("us", [I3, TOK], BF16).ap()
    g_d = nc.dram_tensor("gs", [I3, TOK], BF16).ap()
    wq_d = {"up": nc.dram_tensor("wupq", [I3, E], BF16).ap(),
            "gate": nc.dram_tensor("wgateq", [I3, E], BF16).ap(),
            "down": nc.dram_tensor("wdownq", [E, I3], BF16).ap()}
    CH = NKV * D * TOK
    kvin = nc.dram_tensor("kvin", [2 * CH], BF16).ap()
    kvout_t = nc.dram_tensor("kvout", [4 * CH], BF16)
    kvout = kvout_t.ap()
    mx_i = nc.dram_tensor("mxi", [1, 256], F32).ap()
    mx_o = nc.dram_tensor("mxo", [1, 256], F32).ap()
    bc_t = nc.dram_tensor("bcs", [64], F32)
    bc_d = bc_t.ap()

    P = Prog()
    regbox = {}
    glob = contextlib.ExitStack()
    with glob:
        def gsb(name, shape, dt=F32):
            return glob.enter_context(nc.sbuf_tensor(name, shape, dt))

        sems = {}
        for nm in ("qy0", "qy1", "qs0", "qs1", "qg0", "qg1",
                   "cv", "cs", "ct", "cg", "cc"):
            sems[nm] = glob.enter_context(nc.semaphore(nm))
        sems["qy"] = sems["qy0"]
        sems["qs"] = sems["qs0"]
        sems["qg"] = sems["qg0"]

        iq = gsb("iq", [128, 2048])
        ikg = gsb("ikg", [128, NKC])
        ones64 = gsb("ones64", [128, 64], BF16)
        ones2 = gsb("ones2", [128, 65], BF16)
        bo_s = gsb("bo_s", [128, 8])
        bball = gsb("bball", [128, NH])
        sca = gsb("sca", [128, 16])
        scr1 = gsb("scr1", [1, 32])
        biasm = gsb("biasm", [128, 8 * NKC])
        mqk = gsb("mqk", [128, 16])
        mqkred = gsb("mqkred", [128, 1])
        mq64 = gsb("mq64", [1, 16])
        ffmax = gsb("ffmax", [128, 2 * NI * (TOK // 512)])
        poff_s = gsb("poff_s", [1, 1], mybir.dt.int32)
        clrow = gsb("clrow", [1, 256])
        mgcb = gsb("mgcb", [128, 1])
        parb = gsb("parb", [128, 4])

        def f(g):
            g.iota(iq[:], [[1, 2048]], channel_multiplier=0,
                   allow_small_or_imprecise_dtypes=True)
            g.iota(ikg[:], [[128, NKC]], channel_multiplier=1,
                   allow_small_or_imprecise_dtypes=True)
            r = g.alloc_register()
            regbox["partner"] = r
            r2 = g.alloc_register()
            regbox["tmp"] = r2
            g.dma_start(out=poff_s[:], in_=poff_d).then_inc(sems["qg"], 16)
            g.wait_ge(sems["qg"], 16)
            g.reg_load(r, poff_s[0:1, 0:1])
            return g.engine_nop()
        P.step("g", f, inc=("cg", 1), dma=1)

        def f(v):
            v.memset(ones64[:], 1.0)
            v.memset(ones2[:], 0.0)
            v.memset(mgcb[:], MAGIC)
            v.drain()
            v.memset(ones2[0:64, 0:1], 1.0)
            v.memset(ones2[64:128, 64:65], 1.0)
            return v.drain()
        P.step("v", f, inc=("cv", 1))

        def f(y):
            return y.dma_start(out=bo_s[:], in_=bo_d).then_inc(sems["qy"], 16)
        P.step("y", f, dma=1)


        def phase_barrier():
            w = [("cs", P.cnt("cs")), ("ct", P.cnt("ct")),
                 ("cg", P.cnt("cg")), ("cc", P.cnt("cc"))]
            for qn in ("qy0", "qy1", "qs0", "qs1", "qg0", "qg1"):
                if P.cnt(qn) > 0:
                    w.append((qn, P.cnt(qn)))

            def fb(v):
                return None
            P.step("v", fb, wait=w, inc=("cv", 1))
            bar = P.cnt("cv")
            for en in ("s", "t", "g", "y"):
                def fb2(e):
                    return None
                P.step(en, fb2, wait=[("cv", bar)])

        attsc = contextlib.ExitStack()
        qt_s = attsc.enter_context(nc.sbuf_tensor("qt_s", [128, 8 * TOK], BF16))
        kt_s = attsc.enter_context(nc.sbuf_tensor("kt_s", [128, 4 * S], BF16))
        vt_s = attsc.enter_context(nc.sbuf_tensor("vt_s", [128, NKC * 512], BF16))

        pA = contextlib.ExitStack()

        def asb(name, shape, dt=F32):
            return pA.enter_context(nc.sbuf_tensor(name, shape, dt))

        xr = asb("xr", [128, 4 * E])
        z1r = asb("z1r", [128, 2 * E], BF16)
        mv2 = asb("mv2", [128, 2])
        st12 = asb("st12", [128, 12])
        rr = asb("rr", [128, NTM])
        mbb = asb("mbb", [128, NTM])
        wst = asb("wst", [128, 2 * E])
        wpb = asb("wpb", [128, 8 * E], BF16)
        zf = asb("zf", [128, 8 * TOK], BF16)
        sqq = asb("sqq", [128, 512], BF16)
        pmA = [pA.enter_context(nc.psum_tensor("pmA%d" % i, [128, 512], F32))
               for i in range(2)]

        norm_cs = {}
        z1w_qs = {}
        for i in range(NTM):
            xi = xr[:, (i % 4) * E:(i % 4) * E + E]

            def f(y, xi=xi, i=i):
                return y.dma_start(out=xi, in_=x_d[i * 128:(i + 1) * 128, :]
                                   ).then_inc(sems["qy"], 16)
            P.step("y", f, wait=([("cs", norm_cs[i - 4])] if i >= 4 else []),
                   dma=1)

            def f(v, xi=xi, i=i):
                v.bn_stats(st12[:, 0:6], xi[:, 0:512])
                v.bn_stats(st12[:, 6:12], xi[:, 512:1024])
                v.drain()
                v.bn_aggr(mv2[:], st12[:])
                v.drain()
                return v.tensor_scalar(rr[:, i:i + 1], mv2[:, 1:2], EPS,
                                       None, AL.add)
            P.step("v", f, wait=[P.qlast("y")], inc=("cv", 1))

            def f(s, i=i):
                return s.activation(rr[:, i:i + 1], rr[:, i:i + 1], AF.Sqrt,
                                    bias=0.0, scale=1.0)
            P.step("s", f, wait=[("cv", P.cnt("cv"))], inc=("cs", 1))

            def f(v, i=i):
                v.reciprocal(rr[:, i:i + 1], rr[:, i:i + 1])
                v.drain()
                return v.scalar_tensor_tensor(mbb[:, i:i + 1], mv2[:, 0:1],
                                              -1.0, rr[:, i:i + 1], AL.mult,
                                              AL.mult)
            P.step("v", f, wait=[("cs", P.cnt("cs"))], inc=("cv", 1))

            zsl = z1r[:, (i % 2) * E:(i % 2) * E + E]

            def f(s, i=i, xi=xi, zsl=zsl):
                s.activation(zsl, xi, AF.Identity, bias=mbb[:, i:i + 1],
                             scale=rr[:, i:i + 1])
                s.drain()
                s.dma_start(out=z1_d[i * 128:(i + 1) * 128, :], in_=zsl
                            ).then_inc(sems["qs"], 16)
                return None
            P.step("s", f, wait=[("cv", P.cnt("cv"))] +
                   ([z1w_qs[i - 2]] if i >= 2 else []),
                   inc=("cs", 1), dma=1)
            norm_cs[i] = P.cnt("cs")
            z1w_qs[i] = P.qlast("s")

        def f(s):
            last = None
            for c in range(8):
                last = s.dma_start(out=zf[:, c * TOK:(c + 1) * TOK],
                                   in_=z1_d[:, c * 128:(c + 1) * 128],
                                   transpose=True).then_inc(sems["qs"], 16)
            return last
        P.step("s", f, wait=[P.qlast("s")], dma=8)
        ck_zf = P.qlast("s")

        if "z1" in taps:
            def f(g):
                return g.dma_start(out=tap_d["z1"][:, :], in_=z1_d[:, :]
                                   ).then_inc(sems["qg"], 16)
            P.step("g", f, wait=[ck_zf], dma=1)

        wq_cv = {}
        for i in range(8):
            def f(y, i=i):
                return y.dma_start(out=wst[:, (i % 2) * E:(i % 2) * E + E],
                                   in_=wqt_d[i * 128:(i + 1) * 128, :]
                                   ).then_inc(sems["qy"], 16)
            P.step("y", f, wait=([("cv", wq_cv[i - 2])] if i >= 2 else []),
                   dma=1)

            def f(v, i=i):
                return v.tensor_copy(wpb[:, E * i:E * i + E],
                                     wst[:, (i % 2) * E:(i % 2) * E + E])
            P.step("v", f, wait=[P.qlast("y")], inc=("cv", 1))
            wq_cv[i] = P.cnt("cv")
        ck_wq = P.cnt("cv")

        evac_cv = {}
        nmm = 0
        for o in range(8):
            for qc in range(NQ5):
                bank = pmA[nmm % 2]

                def f(t, o=o, qc=qc, bank=bank):
                    last = None
                    for fc in range(8):
                        last = t.matmul(
                            bank[:],
                            wpb[:, E * fc + 128 * o:E * fc + 128 * o + 128],
                            zf[:, fc * TOK + qc * 512:fc * TOK + qc * 512 + 512],
                            start=(fc == 0), stop=(fc == 7))
                    return last
                w = [("cv", ck_wq), ck_zf]
                if nmm >= 2:
                    w.append(("cv", evac_cv[nmm - 2]))
                P.step("t", f, wait=w, inc=("ct", 1))

                def f(v, o=o, qc=qc, bank=bank):
                    return v.tensor_copy(
                        qt_s[:, o * TOK + qc * 512:o * TOK + qc * 512 + 512],
                        bank[:])
                P.step("v", f, wait=[("ct", P.cnt("ct"))], inc=("cv", 1))
                evac_cv[nmm] = P.cnt("cv")
                nmm += 1
        ck_qproj_t = P.cnt("ct")

        wkv_cv = {}
        for i in range(16):
            src = wkt_d if i < 8 else wvt_d
            fc = i % 8
            dst = 512 * i

            def f(y, i=i, src=src, fc=fc):
                return y.dma_start(out=wst[:, (i % 2) * E:(i % 2) * E + 512],
                                   in_=src[fc * 128:(fc + 1) * 128, :]
                                   ).then_inc(sems["qy"], 16)
            P.step("y", f, wait=[("ct", ck_qproj_t)] +
                   ([("cv", wkv_cv[i - 2])] if i >= 2 else []), dma=1)

            def f(v, i=i, dst=dst):
                return v.tensor_copy(wpb[:, dst:dst + 512],
                                     wst[:, (i % 2) * E:(i % 2) * E + 512])
            P.step("v", f, wait=[P.qlast("y")], inc=("cv", 1))
            wkv_cv[i] = P.cnt("cv")
        ck_wkv = P.cnt("cv")

        for u in range(4):
            for qc in range(NQ5):
                bank = pmA[nmm % 2]

                def f(t, u=u, qc=qc, bank=bank):
                    last = None
                    for fc in range(8):
                        last = t.matmul(
                            bank[:],
                            wpb[:, 512 * fc + 128 * u:512 * fc + 128 * u + 128],
                            zf[:, fc * TOK + qc * 512:fc * TOK + qc * 512 + 512],
                            start=(fc == 0), stop=(fc == 7))
                    return last
                P.step("t", f, wait=[("cv", max(ck_wkv, evac_cv[nmm - 2]))],
                       inc=("ct", 1))

                def f(v, u=u, qc=qc, bank=bank):
                    return v.tensor_copy(
                        kt_s[:, u * S + qc * 512:u * S + qc * 512 + 512],
                        bank[:])
                P.step("v", f, wait=[("ct", P.cnt("ct"))], inc=("cv", 1))
                evac_cv[nmm] = P.cnt("cv")
                nmm += 1

        for tc in range(NTM):
            bank = pmA[nmm % 2]

            def f(t, tc=tc, bank=bank):
                last = None
                for fc in range(8):
                    last = t.matmul(
                        bank[:],
                        zf[:, fc * TOK + tc * 128:fc * TOK + tc * 128 + 128],
                        wpb[:, 512 * (8 + fc):512 * (8 + fc) + 512],
                        start=(fc == 0), stop=(fc == 7))
                return last
            P.step("t", f, wait=[("cv", max(ck_wkv, evac_cv[nmm - 2]))],
                   inc=("ct", 1))

            def f(v, tc=tc, bank=bank):
                return v.tensor_copy(vt_s[:, tc * 512:tc * 512 + 512], bank[:])
            P.step("v", f, wait=[("ct", P.cnt("ct"))], inc=("cv", 1))
            evac_cv[nmm] = P.cnt("cv")
            nmm += 1
        ck_kvproj = P.cnt("cv")

        def f(g):
            last = None
            for u in range(4):
                last = g.dma_start(
                    out=kvin[u * 128 * TOK:(u + 1) * 128 * TOK].rearrange(
                        "(p t) -> p t", p=128),
                    in_=kt_s[:, u * S:u * S + TOK]).then_inc(sems["qg"], 16)
            for tc in range(NTM):
                last = g.dma_start(
                    out=kvin[CH + tc * 128 * 512:CH + (tc + 1) * 128 * 512
                             ].rearrange("(p t) -> p t", p=128),
                    in_=vt_s[:, tc * 512:tc * 512 + 512]).then_inc(sems["qg"], 16)
            return last
        P.step("g", f, wait=[("cv", ck_kvproj)], dma=4 + NTM)

        def f(g):
            return g.collective_compute(
                "AllGather", AL.bypass,
                replica_groups=[[0, 1], [2, 3], [4, 5], [6, 7]],
                ins=[kvin], outs=[kvout.rearrange("(r n) -> r n", r=2)])
        P.step("g", f, wait=[P.qlast("g")], inc=("cc", 1))
        ck_ag = P.cnt("cc")

        for u in range(4):
            def f(g, u=u):
                r, r2 = regbox["partner"], regbox["tmp"]
                g.reg_mov(r2, u * 128 * TOK)
                g.reg_alu(r2, r2, r, AL.add)
                src = bass.AP(kvout_t, r2, [[TOK, 128], [1, TOK]])
                return g.dma_start(out=kt_s[:, u * S + TOK:u * S + S],
                                   in_=src).then_inc(sems["qg"], 16)
            P.step("g", f, wait=[("cc", ck_ag)], dma=1)
        for tc in range(NTM):
            def f(g, tc=tc):
                r, r2 = regbox["partner"], regbox["tmp"]
                g.reg_mov(r2, CH + tc * 128 * 512)
                g.reg_alu(r2, r2, r, AL.add)
                src = bass.AP(kvout_t, r2, [[512, 128], [1, 512]])
                return g.dma_start(
                    out=vt_s[:, (NTM + tc) * 512:(NTM + tc + 1) * 512],
                    in_=src).then_inc(sems["qg"], 16)
            P.step("g", f, wait=[("cc", ck_ag)], dma=1)
        ck_kv_g = P.qlast("g")
        ck_kv_v = P.cnt("cv")

        if "kt" in taps:
            def f(g):
                return g.dma_start(out=tap_d["kt"][:, :], in_=kt_s[:]
                                   ).then_inc(sems["qg"], 16)
            P.step("g", f, wait=[("cv", ck_kv_v)], dma=1)
        if "vt" in taps:
            def f(g):
                return g.dma_start(out=tap_d["vt"][:, :], in_=vt_s[:]
                                   ).then_inc(sems["qg"], 16)
            P.step("g", f, wait=[ck_kv_g], dma=1)
        if "qt" in taps:
            def f(g):
                return g.dma_start(out=tap_d["qt"][:, :], in_=qt_s[:]
                                   ).then_inc(sems["qg"], 16)
            P.step("g", f, wait=[("cv", P.cnt("cv"))], dma=1)

        def emit_norm(src, colbase, width, dst_col, part_lo=None):
            for ch in range(width // 512):
                if part_lo is not None and ch >= part_lo:
                    wr = [("cv", ck_kv_v), ck_kv_g]
                else:
                    wr = [("cv", ck_kvproj)]
                def f(s, src=src, colbase=colbase, ch=ch):
                    s.activation(sqq[:],
                                 src[:, colbase + ch * 512:colbase + ch * 512 + 512],
                                 AF.Square, bias=0.0, scale=1.0)
                    return s.drain()
                P.step("s", f, wait=wr + [("ct", P.cnt("ct")),
                                          ("cv", P.cnt("cv"))],
                       inc=("cs", 1))

                def f(t):
                    return t.matmul(pmA[0][0:65, :], ones2[:], sqq[:],
                                    start=True, stop=True)
                P.step("t", f, wait=[("cs", P.cnt("cs"))], inc=("ct", 1))

                def f(v, dst_col=dst_col, ch=ch):
                    v.tensor_reduce(mqkred[0:65, 0:1], pmA[0][0:65, :], AX.X,
                                    AL.max)
                    v.drain()
                    if ch == 0:
                        v.tensor_copy(mqk[0:65, dst_col:dst_col + 1],
                                      mqkred[0:65, 0:1])
                    else:
                        v.tensor_tensor(mqk[0:65, dst_col:dst_col + 1],
                                        mqk[0:65, dst_col:dst_col + 1],
                                        mqkred[0:65, 0:1], AL.max)
                    return v.drain()
                P.step("v", f, wait=[("ct", P.cnt("ct"))], inc=("cv", 1))

        for p in range(8):
            emit_norm(qt_s, p * TOK, TOK, p)
        for u in range(4):
            emit_norm(kt_s, u * S, S, 8 + u, part_lo=TOK // 512)

        def f(v):
            v.tensor_copy(mq64[0:1, 0:12], mqk[64:65, 0:12])
            v.drain()
            for p in range(8):
                for j in range(2):
                    h = PAIRS[p][j]
                    kv = h // 2
                    u, j2 = kv // 2, kv % 2
                    a = mqk[0:1, p:p + 1] if j == 0 else mq64[0:1, p:p + 1]
                    b = (mqk[0:1, 8 + u:9 + u] if j2 == 0
                         else mq64[0:1, 8 + u:9 + u])
                    v.tensor_tensor(scr1[0:1, h:h + 1], a, b, AL.mult)
            return v.drain()
        P.step("v", f, wait=[("cv", P.cnt("cv"))], inc=("cv", 1))

        def f(s):
            s.activation(scr1[0:1, 0:16], scr1[0:1, 0:16], AF.Sqrt,
                         bias=0.0, scale=1.0)
            return s.drain()
        P.step("s", f, wait=[("cv", P.cnt("cv"))], inc=("cs", 1))

        def f(v):
            v.tensor_scalar(scr1[0:1, 0:16], scr1[0:1, 0:16], 1.0 / 8.0, None,
                            AL.mult)
            return v.drain()
        P.step("v", f, wait=[("cs", P.cnt("cs"))], inc=("cv", 1))

        def f(g):
            return g.dma_start(out=bc_d[0:16].rearrange("(o n) -> o n", o=1),
                               in_=scr1[0:1, 0:16]).then_inc(sems["qg"], 16)
        P.step("g", f, wait=[("cv", P.cnt("cv"))], dma=1)

        def f(g):
            return g.dma_start(out=bball[:],
                               in_=bass.AP(bc_t, 0, [[0, 128], [1, 16]])
                               ).then_inc(sems["qg"], 16)
        P.step("g", f, wait=[P.qlast("g")], dma=1)

        def f(g):
            return g.engine_nop()
        P.step("g", f, wait=[P.qlast("g")], inc=("cg", 1))
        ck_bball = P.cnt("cg")

        if "bb" in taps:
            def f(g):
                return g.dma_start(out=tap_d["bb"][:, :], in_=bball[:]
                                   ).then_inc(sems["qg"], 16)
            P.step("g", f, wait=[("cg", ck_bball)], dma=1)

        pA.close()
        phase_barrier()

        pC = contextlib.ExitStack()

        def csb(name, shape, dt=F32):
            return pC.enter_context(nc.sbuf_tensor(name, shape, dt))

        att = csb("att", [128, 8 * TOK], BF16)
        wob = csb("wob", [128, 8 * E], BF16)
        lpq = csb("lpq", [128, 2 * 1024])
        lps = csb("lps", [128, 4 * 48])
        lpr = csb("lpr", [128, 4 * 48])
        lpb = csb("lpb", [128, 2 * 1024], BF16)
        apr = csb("apr", [128, 2 * 512], BF16)

        pC1 = contextlib.ExitStack()

        def c1sb(name, shape, dt=F32):
            return pC1.enter_context(nc.sbuf_tensor(name, shape, dt))

        QLM = 1024 if TOK >= 1024 else 512
        sc_b = [c1sb("scb%d" % j, [128, 2 * QLM], BF16) for j in range(2)]
        tA = c1sb("tA", [128, 2 * 512], BF16)
        tev = c1sb("tev", [128, 2 * 512])
        rcp = c1sb("rcp", [128, 512])
        prc = c1sb("prc", [128, 2])
        gdg = c1sb("gdg", [128, 2 * (2 * QLM - 128)], BF16)
        gdgf = c1sb("gdgf", [128, 2 * (2 * QLM - 128)], BF16)
        gev = c1sb("gev", [128, 2 * QLM], BF16)
        pe3 = [pC1.enter_context(nc.psum_tensor("pe%d" % i, [128, QLM], F32))
               for i in range(3)]
        pvb = [[pC1.enter_context(
            nc.psum_tensor("pv%d%d" % (j, hh), [128, 512], F32))
            for hh in range(QLM // 512)] for j in range(1)]

        def sc_ap(sslot, off, width):
            return sc_b[sslot // 2][:, (sslot % 2) * QLM + off:
                                    (sslot % 2) * QLM + off + width]

        bg_work = []
        nbg = 0
        for kind, n_t, blkw in (("up", NI, E), ("gate", NI, E),
                                ("down", 8, I3)):
            src = {"up": wup_d, "gate": wgate_d, "down": wdown_d}[kind]
            nsub = blkw // 1024 if blkw > 1024 else 1
            subw = min(blkw, 1024)
            for ti in range(n_t):
                for si in range(nsub):
                    q2 = (nbg % 2) * 1024
                    s4 = (nbg % 4) * 48
                    nbg += 1

                    def fl(y, src=src, ti=ti, si=si, subw=subw, q2=q2):
                        return y.dma_start(
                            out=lpq[:, q2:q2 + subw],
                            in_=src[ti * 128:(ti + 1) * 128,
                                    si * subw:(si + 1) * subw]
                            ).then_inc(sems["qy"], 16)

                    def fd(v, subw=subw, q2=q2, s4=s4):
                        nblk = subw // 32
                        sl3 = lpq[:, q2:q2 + subw].rearrange(
                            "p (b j) -> p b j", j=32)
                        v.tensor_reduce(lps[:, s4:s4 + nblk], sl3, AX.X,
                                        AL.max, apply_absolute_value=True)
                        v.drain()
                        v.tensor_scalar(lps[:, s4:s4 + nblk],
                                        lps[:, s4:s4 + nblk],
                                        1.0 / 7.0, 1e-12, AL.mult, AL.max)
                        v.drain()
                        v.reciprocal(lpr[:, s4:s4 + nblk],
                                     lps[:, s4:s4 + nblk])
                        v.drain()
                        rb = bass.AP(lpr, s4, [[192, 128], [1, nblk], [0, 32]])
                        v.tensor_tensor(sl3, sl3, rb, AL.mult)
                        v.drain()
                        v.tensor_scalar(sl3, sl3, MAGIC, MAGIC, AL.add,
                                        AL.subtract)
                        v.drain()
                        sb3 = lpb[:, q2:q2 + subw].rearrange(
                            "p (b j) -> p b j", j=32)
                        scb = bass.AP(lps, s4, [[192, 128], [1, nblk], [0, 32]])
                        v.tensor_tensor(sb3, sl3, scb, AL.mult)
                        return v.drain()

                    def fw(g, kind=kind, ti=ti, si=si, subw=subw, q2=q2):
                        return g.dma_start(
                            out=wq_d[kind][ti * 128:(ti + 1) * 128,
                                           si * subw:(si + 1) * subw],
                            in_=lpb[:, q2:q2 + subw]).then_inc(sems["qg"], 16)
                    bg_work.append(("lp", fl, fd, fw))
        for i in range(8):
            q2 = (nbg % 2) * 1024
            nbg += 1

            def fl(y, i=i, q2=q2):
                return y.dma_start(out=lpq[:, q2:q2 + 1024],
                                   in_=wot_d[i * 128:(i + 1) * 128, :]
                                   ).then_inc(sems["qy"], 16)

            def fc(v, i=i, q2=q2):
                v.tensor_copy(wob[:, E * i:E * i + E], lpq[:, q2:q2 + 1024])
                return v.drain()
            bg_work.append(("wo", fl, fc))

        bg_iter = iter(bg_work)
        bg_state = {"n": 0,
                    "lpq": [None, None],
                    "lps": [None, None, None, None],
                    "lpb": [None, None]}

        def emit_bg(nslots):
            for _ in range(nslots):
                it = next(bg_iter, None)
                if it is None:
                    return
                n = bg_state["n"]
                bg_state["n"] = n + 1
                sq = n % 2
                s4i = n % 4
                if it[0] == "lp":
                    _, fl, fd, fw = it
                    wl = ([bg_state["lpq"][sq]]
                          if bg_state["lpq"][sq] else [])
                    P.step("y", fl, wait=wl, dma=1)
                    wv = [P.qlast("y")]
                    if bg_state["lpb"][sq] is not None:
                        wv.append(bg_state["lpb"][sq])
                    P.step("v", fd, wait=wv, inc=("cv", 1))
                    bg_state["lpq"][sq] = ("cv", P.cnt("cv"))
                    P.step("g", fw, wait=[("cv", P.cnt("cv"))], dma=1)
                    bg_state["lpb"][sq] = P.qlast("g")
                else:
                    _, fl, fc = it
                    wl = ([bg_state["lpq"][sq]]
                          if bg_state["lpq"][sq] else [])
                    P.step("y", fl, wait=wl, dma=1)
                    P.step("v", fc, wait=[P.qlast("y")], inc=("cv", 1))
                    bg_state["lpq"][sq] = ("cv", P.cnt("cv"))


        exp_done = {}
        pv_done = {}
        exp_bank = {}
        pv_slot = {}
        ev_prev = 0
        ev_bank = 0
        HL = [(p, j) for p in range(8) for j in range(2)]
        pg_done = {}

        def emit_pairgen(idx):
            p, j = HL[idx]
            h = PAIRS[p][j]
            sl = _slope(h)
            qbl = qblocks(p)
            QL = qbl[0][1]
            D0 = QL - 128
            pr2 = idx % 2
            gbase = pr2 * (2 * QLM - 128)
            vbase = pr2 * QLM
            bbase = pr2 * (4 * NKC)
            w0 = [("cg", ck_bball), ("cv", ev_prev),
                  ("cs", exp_done.get("last", 0))]

            def f(v, h=h, sl=sl, qbl=qbl, bbase=bbase):
                last = None
                for qbi, (q0, ql) in enumerate(qbl):
                    bnd = min(NKC // 2, (q0 + ql) // 128)
                    base = bbase + qbi * NKC
                    slices = ((0, bnd, float(-q0), 1.0),
                              (bnd, NKC // 2, float(-q0), -1.0),
                              (NKC // 2, NKC,
                               float(q0 - (S - 1 + TOK)), 1.0))
                    for lo, hi, c0, sgn in slices:
                        if lo >= hi:
                            continue
                        v.tensor_scalar(biasm[:, base + lo:base + hi],
                                        ikg[:, lo:hi], c0,
                                        float(sgn * sl / 8.0),
                                        AL.add, AL.mult)
                        v.drain()
                        last = v.tensor_scalar(
                            biasm[:, base + lo:base + hi],
                            biasm[:, base + lo:base + hi],
                            bball[:, h:h + 1], None, AL.subtract)
                        v.drain()
                return last
            P.step("v", f, wait=w0, inc=("cv", 1))

            def f(v, sl=sl, D0=D0, pr2=pr2):
                return v.tensor_scalar(prc[:, pr2:pr2 + 1], ikg[:, 0:1],
                                       float(D0), float(-2.0 * sl / 8.0),
                                       AL.add, AL.mult)
            P.step("v", f, wait=[("cv", P.cnt("cv"))], inc=("cv", 1))

            def f(s, sl=sl, QL=QL, pr2=pr2, gbase=gbase, vbase=vbase):
                s.activation(gdgf[:, gbase:gbase + 2 * QL - 128],
                             iq[:, 0:2 * QL - 128], AF.Exp,
                             bias=prc[:, pr2:pr2 + 1], scale=2.0 * sl / 8.0)
                s.drain()
                return s.activation(gev[:, vbase:vbase + QL], iq[:, 0:QL],
                                    AF.Exp, bias=0.0, scale=2.0 * sl / 8.0)
            P.step("s", f, wait=[("cv", P.cnt("cv"))], inc=("cs", 1))

            def f(v, QL=QL, gbase=gbase):
                return v.tensor_scalar(gdg[:, gbase:gbase + 2 * QL - 128],
                                       gdgf[:, gbase:gbase + 2 * QL - 128],
                                       1.0, 1.0, AL.mult, AL.min)
            P.step("v", f, wait=[("cs", P.cnt("cs"))], inc=("cv", 1))
            pg_done[idx] = P.cnt("cv")

        for idx, (p, j) in enumerate(HL):
            if idx == 0:
                emit_pairgen(0)
            if idx + 1 < len(HL):
                emit_pairgen(idx + 1)
            if True:
                h = PAIRS[p][j]
                u = p // 2
                g_kv = 2 * u + j
                sl = _slope(h)
                qbl = qblocks(p)
                QL = qbl[0][1]
                D0 = QL - 128
                jb = 64 * j
                pr2 = idx % 2
                gbase = pr2 * (2 * QLM - 128)
                vbase = pr2 * QLM
                bbase = pr2 * (4 * NKC)
                ck_pairgen = pg_done[idx]

                for qbi, (q0, ql) in enumerate(qbl):
                    bnd = min(NKC // 2, (q0 + ql) // 128)
                    d0c = q0 // 128
                    nq5 = ql // 512
                    kcs = list(range(NKC))
                    nkl = len(kcs)
                    pend_pv = None
                    for ki in range(nkl + 1):
                      if ki < nkl:
                        kc = kcs[ki]
                        reg = "A" if kc < bnd else "B"
                        isdiag = (d0c <= kc < bnd)
                        eset = ki % 3
                        sslot = ki % 4

                        def f(t, u=u, kc=kc, q0=q0, nq5=nq5, p=p, jb=jb,
                              eset=eset):
                            last = None
                            for hf in range(nq5):
                                last = t.matmul(
                                    pe3[eset][:, hf * 512:hf * 512 + 512],
                                    kt_s[jb:jb + 64,
                                         u * S + kc * 128:u * S + kc * 128 + 128],
                                    qt_s[jb:jb + 64,
                                         p * TOK + q0 + hf * 512:
                                         p * TOK + q0 + hf * 512 + 512],
                                    start=True, stop=True,
                                    tile_position=(jb, 0))
                            return last
                        w = [("cv", ck_pairgen)]
                        if ki >= 3:
                            w.append(("cs", exp_done[(qbi, kcs[ki - 3])]))
                        elif exp_bank.get(eset, 0) > 0:
                            w.append(("cs", exp_bank[eset]))
                        P.step("t", f, wait=w, inc=("ct", 1))
                        ck_energy = P.cnt("ct")

                      if pend_pv is not None:
                          fn_pv, w_pv, kprev, kiprev = pend_pv
                          P.step("t", fn_pv, wait=w_pv, inc=("ct", 1))
                          pv_done[(qbi, kprev)] = P.cnt("ct")
                          pv_done["last"] = P.cnt("ct")
                          pv_slot[kiprev % 4] = P.cnt("ct")
                          pend_pv = None
                      if ki == nkl:
                          break

                      def f(s, kc=kc, ql=ql, qbi=qbi, sslot=sslot,
                            eset=eset, bbase=bbase):
                          return s.activation(
                              sc_ap(sslot, 0, ql),
                              pe3[eset][:, 0:ql], AF.Exp,
                              bias=biasm[:, bbase + qbi * NKC + kc:
                                         bbase + qbi * NKC + kc + 1],
                              scale=0.125)
                      w = [("ct", ck_energy)]
                      if ki >= 4:
                          w.append(("ct", pv_done[(qbi, kcs[ki - 4])]))
                      elif pv_slot.get(sslot, 0) > 0:
                          w.append(("ct", pv_slot[sslot]))
                      P.step("s", f, wait=w, inc=("cs", 1))
                      exp_done[(qbi, kc)] = P.cnt("cs")
                      exp_done["last"] = P.cnt("cs")
                      exp_bank[eset] = P.cnt("cs")

                      ck_corr = None
                      if isdiag:
                          off = D0 - (kc * 128 - q0)

                          def f(v, off=off, ql=ql, sslot=sslot, gbase=gbase):
                              return v.tensor_tensor(
                                  sc_ap(sslot, 0, ql),
                                  sc_ap(sslot, 0, ql),
                                  gdg[:, gbase + off:gbase + off + ql],
                                  AL.mult)
                          P.step("v", f, wait=[("cs", exp_done[(qbi, kc)])],
                                 inc=("cv", 1))
                          ck_corr = P.cnt("cv")

                      first_A = (ki == 0)
                      last_A = (kc == bnd - 1)
                      first_B = (ki == bnd)
                      last_B = (kc == NKC - 1)

                      ck_snap = None
                      if reg == "B" and first_B:
                          for hf in range(nq5):
                              def f(v, hf=hf):
                                  return v.tensor_copy(
                                      tA[:, hf * 512:hf * 512 + 512],
                                      pvb[0][hf][:])
                              P.step("v", f,
                                     wait=[("ct", pv_done[(qbi, kcs[ki - 1])])],
                                     inc=("cv", 1))
                          ck_snap = P.cnt("cv")

                      def f(t, g_kv=g_kv, kc=kc, nq5=nq5, sslot=sslot,
                            reg=reg, first_A=first_A, last_A=last_A,
                            first_B=first_B, last_B=last_B):
                          st = first_A if reg == "A" else first_B
                          sp = last_A if reg == "A" else last_B
                          last = None
                          for hf in range(nq5):
                              rhs = sc_ap(sslot, hf * 512, 512)
                              bank = pvb[0][hf]
                              last = t.matmul(
                                  bank[0:64, :],
                                  vt_s[:, kc * 512 + g_kv * 64:
                                       kc * 512 + g_kv * 64 + 64],
                                  rhs, start=st, stop=sp,
                                  tile_position=(0, 0),
                                  skip_group_check=True)
                              last = t.matmul(
                                  bank[64:128, :], ones64[:], rhs,
                                  start=st, stop=sp, tile_position=(0, 64),
                                  skip_group_check=True)
                          return last
                      w = [("cs", exp_done[(qbi, kc)])]
                      if ck_corr is not None:
                          w.append(("cv", ck_corr))
                      if ck_snap is not None:
                          w.append(("cv", ck_snap))
                      if first_A:
                          w.append(("cv", ev_bank))
                      pend_pv = (f, w, kc, ki)

                    for hf in range(nq5):
                        bank = pvb[0][hf]
                        tsl = tev[:, hf * 512:hf * 512 + 512]
                        asl = tA[:, hf * 512:hf * 512 + 512]

                        def f(v, hf=hf, bank=bank, tsl=tsl, asl=asl,
                              vbase=vbase):
                            v.scalar_tensor_tensor(
                                tsl, bank[:], 1.0,
                                gev[:, vbase + hf * 512:vbase + hf * 512 + 512],
                                AL.mult, AL.mult)
                            v.drain()
                            return v.tensor_tensor(tsl, tsl, asl, AL.add)
                        P.step("v", f, wait=[("ct", pv_done[(qbi, NKC - 1)])],
                               inc=("cv", 1))

                        def f(s, hf=hf, tsl=tsl):
                            s.activation(rcp[0:64, 0:512],
                                         tsl[64:128, :], AF.Ln, bias=0.0,
                                         scale=1.0)
                            s.drain()
                            return s.activation(
                                rcp[0:64, 0:512],
                                rcp[0:64, 0:512],
                                AF.Exp, bias=0.0, scale=-1.0)
                        P.step("s", f, wait=[("cv", P.cnt("cv"))], inc=("cs", 1))

                        def f(v, p=p, jb=jb, hf=hf, q0=q0, tsl=tsl):
                            return v.tensor_tensor(
                                att[jb:jb + 64,
                                    p * TOK + q0 + hf * 512:
                                    p * TOK + q0 + hf * 512 + 512],
                                tsl[0:64, :],
                                rcp[0:64, 0:512], AL.mult)
                        P.step("v", f, wait=[("cs", P.cnt("cs"))], inc=("cv", 1))
                    ev_prev = P.cnt("cv")
                    ev_bank = P.cnt("cv") - 1
                    emit_bg(2)

        ck_att_v = P.cnt("cv")

        if "att" in taps:
            def f(g):
                return g.dma_start(out=tap_d["att"][:, :], in_=att[:]
                                   ).then_inc(sems["qg"], 16)
            P.step("g", f, wait=[("cv", ck_att_v)], dma=1)

        while True:
            before = len(P.steps)
            emit_bg(4)
            if len(P.steps) == before:
                break
        ck_bg_qy = P.qlast("g")
        ck_bg_cv = P.cnt("cv")

        if "wq" in taps:
            def f(g):
                return g.dma_start(out=tap_d["wq"][:, :], in_=wq_d["up"][:, :]
                                   ).then_inc(sems["qg"], 16)
            P.step("g", f, wait=[ck_bg_qy], dma=1)

        pC1.close()
        phase_barrier()

        pD = contextlib.ExitStack()

        def dsb(name, shape, dt=F32):
            return pD.enter_context(nc.sbuf_tensor(name, shape, dt))

        apev = {}
        apw = {}
        ck_apf_qc = {}
        nmm2 = 0
        poA = pD.enter_context(nc.psum_tensor("poA", [128, 512], F32))
        poB = pD.enter_context(nc.psum_tensor("poB", [128, 512], F32))
        for qc in range(NQ5):
            for o in range(8):
                bank = (poA, poB)[nmm2 % 2]

                def f(t, o=o, qc=qc, bank=bank):
                    last = None
                    for fc in range(8):
                        last = t.matmul(
                            bank[:],
                            wob[:, E * fc + 128 * o:E * fc + 128 * o + 128],
                            att[:, fc * TOK + qc * 512:fc * TOK + qc * 512 + 512],
                            start=(fc == 0), stop=(fc == 7))
                    return last
                w = [("cv", max(ck_att_v, ck_bg_cv))]
                if nmm2 >= 2:
                    w.append(("cs", apev[nmm2 - 2]))
                P.step("t", f, wait=w, inc=("ct", 1))

                def f(s, o=o, qc=qc, bank=bank, nmm2=nmm2):
                    sl = apr[:, (nmm2 % 2) * 512:(nmm2 % 2) * 512 + 512]
                    s.activation(sl, bank[:], AF.Identity,
                                 bias=bo_s[:, o:o + 1], scale=1.0)
                    s.drain()
                    s.dma_start(
                        out=apf_d[o * 128:(o + 1) * 128, qc * 512:qc * 512 + 512],
                        in_=sl).then_inc(sems["qs"], 16)
                    return None
                ws = [("ct", P.cnt("ct"))]
                if nmm2 >= 2:
                    ws.append(apw[nmm2 - 2])
                if o == 7:
                    op = "qs%d" % (1 - P.qpar.get("s", 0))
                    if P.cnt(op) > 0:
                        ws.append((op, P.cnt(op)))
                P.step("s", f, wait=ws, inc=("cs", 1), dma=1)
                apev[nmm2] = P.cnt("cs")
                apw[nmm2] = P.qlast("s")
                nmm2 += 1
            ck_apf_qc[qc] = P.qlast("s")
        ck_apf = P.qlast("s")

        if "apf" in taps:
            def f(g):
                return g.dma_start(out=tap_d["apf"][:, :], in_=apf_d[:, :]
                                   ).then_inc(sems["qg"], 16)
            P.step("g", f, wait=[("qs", ck_apf)], dma=1)

        aptm = dsb("aptm", [128, 3 * E], BF16)
        xr2 = dsb("xr2", [128, 2 * E])
        x1r = dsb("x1r", [128, 3 * E])
        z3r = dsb("z3r", [128, 2 * E], BF16)
        mv2d = dsb("mv2d", [128, 12])
        st12d = dsb("st12d", [128, 12])
        rrd = dsb("rrd", [128, 6])
        mbd = dsb("mbd", [128, 6])
        zmaxc = dsb("zmaxc", [128, NTM])

        x1w = {}
        z3w = {}
        for i in range(NTM):
            sl3 = i % 3
            sl2 = i % 2
            r0 = 2 * sl3
            r1 = 2 * sl3 + 1
            m0 = 4 * sl3
            m2 = 4 * sl3 + 2

            def f(y, i=i, sl2=sl2, sl3=sl3):
                y.dma_start(out=aptm[:, sl3 * E:sl3 * E + E],
                            in_=apf_d[:, i * 128:i * 128 + 128],
                            transpose=True).then_inc(sems["qy"], 16)
                return y.dma_start(out=xr2[:, sl2 * E:sl2 * E + E],
                                   in_=x_d[i * 128:(i + 1) * 128, :]
                                   ).then_inc(sems["qy"], 16)
            w = [ck_apf_qc[i // 4]]
            if i >= 2:
                w.append(z3w[i - 2])
                w.append(x1w[i - 2])
            P.step("y", f, wait=w, dma=2)

            def f(v, sl3=sl3, r0=r0, m0=m0):
                a = aptm[:, sl3 * E:sl3 * E + E]
                v.bn_stats(st12d[:, 0:6], a[:, 0:512])
                v.bn_stats(st12d[:, 6:12], a[:, 512:1024])
                v.drain()
                v.bn_aggr(mv2d[:, m0:m0 + 2], st12d[:])
                v.drain()
                return v.tensor_scalar(rrd[:, r0:r0 + 1],
                                       mv2d[:, m0 + 1:m0 + 2], EPS, None,
                                       AL.add)
            P.step("v", f, wait=[P.qlast("y")], inc=("cv", 1))

            def f(s, r0=r0):
                return s.activation(rrd[:, r0:r0 + 1], rrd[:, r0:r0 + 1],
                                    AF.Sqrt, bias=0.0, scale=1.0)
            P.step("s", f, wait=[("cv", P.cnt("cv"))], inc=("cs", 1))

            def f(v, r0=r0, m0=m0):
                v.reciprocal(rrd[:, r0:r0 + 1], rrd[:, r0:r0 + 1])
                v.drain()
                return v.scalar_tensor_tensor(mbd[:, r0:r0 + 1],
                                              mv2d[:, m0:m0 + 1], -1.0,
                                              rrd[:, r0:r0 + 1], AL.mult,
                                              AL.mult)
            P.step("v", f, wait=[("cs", P.cnt("cs"))], inc=("cv", 1))

            def f(s, sl3=sl3, r0=r0):
                return s.activation(x1r[:, sl3 * E:sl3 * E + E],
                                    aptm[:, sl3 * E:sl3 * E + E], AF.Identity,
                                    bias=mbd[:, r0:r0 + 1],
                                    scale=rrd[:, r0:r0 + 1])
            P.step("s", f, wait=[("cv", P.cnt("cv"))], inc=("cs", 1))

            def f(v, i=i, sl2=sl2, sl3=sl3, r1=r1, m2=m2):
                x1t = x1r[:, sl3 * E:sl3 * E + E]
                v.tensor_tensor(x1t, x1t, xr2[:, sl2 * E:sl2 * E + E], AL.add)
                v.drain()
                v.bn_stats(st12d[:, 0:6], x1t[:, 0:512])
                v.bn_stats(st12d[:, 6:12], x1t[:, 512:1024])
                v.drain()
                v.bn_aggr(mv2d[:, m2:m2 + 2], st12d[:])
                v.drain()
                return v.tensor_scalar(rrd[:, r1:r1 + 1],
                                       mv2d[:, m2 + 1:m2 + 2], EPS, None,
                                       AL.add)
            P.step("v", f, wait=[("cs", P.cnt("cs"))], inc=("cv", 1))

            def f(g, i=i, sl3=sl3):
                return g.dma_start(out=x1_d[i * 128:(i + 1) * 128, :],
                                   in_=x1r[:, sl3 * E:sl3 * E + E]
                                   ).then_inc(sems["qg"], 16)
            P.step("g", f, wait=[("cv", P.cnt("cv"))], dma=1)
            x1w[i] = P.qlast("g")

            def f(s, r1=r1):
                return s.activation(rrd[:, r1:r1 + 1], rrd[:, r1:r1 + 1],
                                    AF.Sqrt, bias=0.0, scale=1.0)
            P.step("s", f, wait=[("cv", P.cnt("cv"))], inc=("cs", 1))

            def f(v, r1=r1, m2=m2):
                v.reciprocal(rrd[:, r1:r1 + 1], rrd[:, r1:r1 + 1])
                v.drain()
                return v.scalar_tensor_tensor(mbd[:, r1:r1 + 1],
                                              mv2d[:, m2:m2 + 1], -1.0,
                                              rrd[:, r1:r1 + 1], AL.mult,
                                              AL.mult)
            P.step("v", f, wait=[("cs", P.cnt("cs"))], inc=("cv", 1))

            def f(s, i=i, sl2=sl2, sl3=sl3, r1=r1):
                s.activation(z3r[:, sl2 * E:sl2 * E + E],
                             x1r[:, sl3 * E:sl3 * E + E], AF.Identity,
                             bias=mbd[:, r1:r1 + 1], scale=rrd[:, r1:r1 + 1])
                s.drain()
                s.dma_start(out=z3_d[i * 128:(i + 1) * 128, :],
                            in_=z3r[:, sl2 * E:sl2 * E + E]
                            ).then_inc(sems["qs"], 16)
                return None
            P.step("s", f, wait=[("cv", P.cnt("cv"))], inc=("cs", 1), dma=1)
            z3w[i] = P.qlast("s")

            def f(v, i=i, sl2=sl2):
                return v.tensor_reduce(zmaxc[:, i:i + 1],
                                       z3r[:, sl2 * E:sl2 * E + E],
                                       AX.X, AL.max,
                                       apply_absolute_value=True)
            P.step("v", f, wait=[("cs", P.cnt("cs"))], inc=("cv", 1))
        ck_z3 = P.qlast("s")
        ck_x1 = P.qlast("g")

        if "x1" in taps:
            def f(g):
                return g.dma_start(out=tap_d["x1"][:, :], in_=x1_d[:, :]
                                   ).then_inc(sems["qg"], 16)
            P.step("g", f, wait=[ck_x1], dma=1)
        if "z3" in taps:
            def f(g):
                return g.dma_start(out=tap_d["z3"][:, :], in_=z3_d[:, :]
                                   ).then_inc(sems["qg"], 16)
            P.step("g", f, wait=[ck_z3], dma=1)

        def allreduce_max(tag, src_col_ap, n, dst_off, wait_cv):
            def f(g, n=n):
                last = None
                for c in range(n):
                    last = g.dma_start(
                        out=mx_i[0:1, 128 * c:128 * c + 128].rearrange(
                            "o (p w) -> (o p) w", w=1),
                        in_=src_col_ap[:, c:c + 1]).then_inc(sems["qg"], 16)
                return last
            P.step("g", f, wait=wait_cv, dma=n)

            def f(g, n=n):
                return g.collective_compute(
                    "AllReduce", AL.max, replica_groups=[list(range(8))],
                    ins=[mx_i[0:1, 0:128 * n]], outs=[mx_o[0:1, 0:128 * n]])
            P.step("g", f, wait=[P.qlast("g")], inc=("cc", 1))

            def f(g, n=n):
                return g.dma_start(out=clrow[0:1, 0:128 * n],
                                   in_=mx_o[0:1, 0:128 * n]
                                   ).then_inc(sems["qg"], 16)
            P.step("g", f, wait=[("cc", P.cnt("cc"))], dma=1)

            def f(v, n=n, dst_off=dst_off):
                for c in range(n):
                    v.tensor_reduce(scr1[0:1, dst_off + c:dst_off + c + 1],
                                    clrow[0:1, 128 * c:128 * c + 128],
                                    AX.X, AL.max)
                    v.drain()
                return v.engine_nop()
            P.step("v", f, wait=[P.qlast("g")], inc=("cv", 1))
            return [("cv", P.cnt("cv"))]

        def f(v):
            v.tensor_reduce(zmaxc[:, 0:1], zmaxc[:, 0:NTM], AX.X, AL.max)
            return v.drain()
        P.step("v", f, wait=[("cv", P.cnt("cv"))], inc=("cv", 1))
        w_ar = allreduce_max("z", zmaxc, 1, 16, [("cv", P.cnt("cv"))])

        def f(v):
            v.tensor_scalar(scr1[0:1, 17:18], scr1[0:1, 16:17], 1.0 / QMAX,
                            1e-12, AL.mult, AL.max)
            v.drain()
            v.reciprocal(scr1[0:1, 18:19], scr1[0:1, 17:18])
            return v.drain()
        P.step("v", f, wait=w_ar, inc=("cv", 1))

        def f(g):
            g.dma_start(out=bc_d[0:1].rearrange("(o n) -> o n", o=1),
                        in_=scr1[0:1, 18:19]).then_inc(sems["qg"], 16)
            return None
        P.step("g", f, wait=[("cv", P.cnt("cv"))], dma=1)

        def f(g):
            return g.dma_start(out=sca[:, 0:1],
                               in_=bass.AP(bc_t, 0, [[0, 128], [1, 1]])
                               ).then_inc(sems["qg"], 16)
        P.step("g", f, wait=[P.qlast("g")], dma=1)

        def f(g):
            return g.engine_nop()
        P.step("g", f, wait=[P.qlast("g")], inc=("cg", 1))
        ck_rz = P.cnt("cg")

        pD.close()
        pC.close()
        attsc.close()
        phase_barrier()

        pE1 = contextlib.ExitStack()

        def e1sb(name, shape, dt=F32):
            return pE1.enter_context(nc.sbuf_tensor(name, shape, dt))

        zf2 = e1sb("zf2", [128, 8 * TOK], BF16)
        zsc = e1sb("zsc", [128, TOK])
        wfu = e1sb("wfu", [128, 8 * I3], BF16)
        wfg = e1sb("wfg", [128, 8 * I3], BF16)
        ffs = e1sb("ffs", [128, 4 * 512], BF16)
        pf = [pE1.enter_context(nc.psum_tensor("pf%d" % i, [128, 512], F32))
              for i in range(2)]

        def f(y):
            last = None
            for c in range(8):
                last = y.dma_start(out=zf2[:, c * TOK:(c + 1) * TOK],
                                   in_=z3_d[:, c * 128:(c + 1) * 128],
                                   transpose=True).then_inc(sems["qy"], 16)
            return last
        P.step("y", f, wait=[ck_z3, ck_bg_qy], dma=8)
        ck_zf2 = P.qlast("y")

        for c in range(8):
            def f(v, c=c):
                v.tensor_scalar(zsc[:], zf2[:, c * TOK:(c + 1) * TOK],
                                sca[:, 0:1], MAGIC, AL.mult, AL.add)
                v.drain()
                return v.tensor_scalar(zf2[:, c * TOK:(c + 1) * TOK], zsc[:],
                                       MAGIC, 1.0, AL.subtract, AL.mult)
            P.step("v", f, wait=[ck_zf2, ("cg", ck_rz),
                                 ("cv", P.cnt("cv"))], inc=("cv", 1))
        ck_zq = P.cnt("cv")

        def f(y):
            last = None
            for fc in range(8):
                last = y.dma_start(out=wfu[:, fc * I3:(fc + 1) * I3],
                                   in_=wq_d["up"][:, fc * 128:(fc + 1) * 128],
                                   transpose=True).then_inc(sems["qy"], 16)
            for fc in range(8):
                last = y.dma_start(out=wfg[:, fc * I3:(fc + 1) * I3],
                                   in_=wq_d["gate"][:, fc * 128:(fc + 1) * 128],
                                   transpose=True).then_inc(sems["qy"], 16)
            return last
        P.step("y", f, wait=[ck_zf2], dma=16)
        ck_wf = P.qlast("y")

        sp_done = {}
        sp_done_q = {}
        nmf = 0
        for kind, wf, dst in (("up", wfu, u_d), ("gate", wfg, g_d)):
            for ic in range(NI):
                for qc in range(NQ5):
                    bank = pf[nmf % 2]

                    def f(t, wf=wf, ic=ic, qc=qc, bank=bank):
                        last = None
                        for fc in range(8):
                            last = t.matmul(
                                bank[:],
                                wf[:, fc * I3 + ic * 128:fc * I3 + ic * 128 + 128],
                                zf2[:, fc * TOK + qc * 512:fc * TOK + qc * 512 + 512],
                                start=(fc == 0), stop=(fc == 7))
                        return last
                    w = [("cv", ck_zq), ck_wf]
                    if nmf >= 2:
                        w.append(("cv", sp_done[nmf - 2]))
                    P.step("t", f, wait=w, inc=("ct", 1))

                    mcol = nmf

                    def f(v, bank=bank, nmf=nmf, mcol=mcol):
                        sl = ffs[:, (nmf % 4) * 512:(nmf % 4) * 512 + 512]
                        v.tensor_reduce(ffmax[:, mcol:mcol + 1], bank[:],
                                        AX.X, AL.max,
                                        apply_absolute_value=True)
                        return v.tensor_copy(sl, bank[:])
                    wv = [("ct", P.cnt("ct"))]
                    if nmf >= 4:
                        wv.append(sp_done_q[nmf - 4])
                    P.step("v", f, wait=wv, inc=("cv", 1))
                    sp_done[nmf] = P.cnt("cv")

                    def f(g, nmf=nmf, dst=dst, ic=ic, qc=qc):
                        sl = ffs[:, (nmf % 4) * 512:(nmf % 4) * 512 + 512]
                        return g.dma_start(
                            out=dst[ic * 128:(ic + 1) * 128,
                                    qc * 512:qc * 512 + 512],
                            in_=sl).then_inc(sems["qg"], 16)
                    P.step("g", f, wait=[("cv", P.cnt("cv"))], dma=1)
                    sp_done_q[nmf] = P.qlast("g")
                    nmf += 1
        ck_s1_v = P.cnt("cv")
        ck_s1_q = P.qlast("g")

        if "uu" in taps:
            def f(g):
                return g.dma_start(out=tap_d["uu"][:, :], in_=u_d[:, :]
                                   ).then_inc(sems["qg"], 16)
            P.step("g", f, wait=[ck_s1_q], dma=1)
        if "gg" in taps:
            def f(g):
                return g.dma_start(out=tap_d["gg"][:, :], in_=g_d[:, :]
                                   ).then_inc(sems["qg"], 16)
            P.step("g", f, wait=[ck_s1_q], dma=1)

        NU = NI * NQ5

        def f(v):
            v.tensor_reduce(ffmax[:, 0:1], ffmax[:, 0:NU], AX.X, AL.max)
            v.drain()
            v.tensor_reduce(ffmax[:, 1:2], ffmax[:, NU:2 * NU], AX.X, AL.max)
            return v.drain()
        P.step("v", f, wait=[("cv", ck_s1_v)], inc=("cv", 1))
        w_ar = allreduce_max("ug", ffmax, 2, 19, [("cv", P.cnt("cv"))])

        def f(v):
            v.tensor_tensor(scr1[0:1, 21:22], scr1[0:1, 19:20],
                            scr1[0:1, 17:18], AL.mult)
            v.drain()
            v.tensor_scalar(scr1[0:1, 21:22], scr1[0:1, 21:22], 1.0 / QMAX,
                            1e-12, AL.mult, AL.max)
            v.drain()
            v.reciprocal(scr1[0:1, 22:23], scr1[0:1, 21:22])
            v.drain()
            v.tensor_tensor(scr1[0:1, 22:23], scr1[0:1, 22:23],
                            scr1[0:1, 17:18], AL.mult)
            v.drain()
            v.tensor_tensor(scr1[0:1, 23:24], scr1[0:1, 20:21],
                            scr1[0:1, 17:18], AL.mult)
            v.drain()
            v.tensor_scalar(scr1[0:1, 23:24], scr1[0:1, 23:24], 1.0 / QMAX,
                            1e-12, AL.mult, AL.max)
            v.drain()
            v.reciprocal(scr1[0:1, 30:31], scr1[0:1, 23:24])
            v.drain()
            v.tensor_tensor(scr1[0:1, 30:31], scr1[0:1, 30:31],
                            scr1[0:1, 17:18], AL.mult)
            return v.drain()
        P.step("v", f, wait=w_ar, inc=("cv", 1))

        def f(g):
            g.dma_start(out=bc_d[0:1].rearrange("(o n) -> o n", o=1),
                        in_=scr1[0:1, 22:23]).then_inc(sems["qg"], 16)
            g.dma_start(out=bc_d[1:2].rearrange("(o n) -> o n", o=1),
                        in_=scr1[0:1, 30:31]).then_inc(sems["qg"], 16)
            g.dma_start(out=bc_d[2:3].rearrange("(o n) -> o n", o=1),
                        in_=scr1[0:1, 23:24]).then_inc(sems["qg"], 16)
            return None
        P.step("g", f, wait=[("cv", P.cnt("cv"))], dma=3)

        def f(g):
            g.dma_start(out=sca[:, 2:3],
                        in_=bass.AP(bc_t, 0, [[0, 128], [1, 1]])
                        ).then_inc(sems["qg"], 16)
            g.dma_start(out=sca[:, 3:4],
                        in_=bass.AP(bc_t, 1, [[0, 128], [1, 1]])
                        ).then_inc(sems["qg"], 16)
            g.dma_start(out=sca[:, 4:5],
                        in_=bass.AP(bc_t, 2, [[0, 128], [1, 1]])
                        ).then_inc(sems["qg"], 16)
            return None
        P.step("g", f, wait=[P.qlast("g")], dma=3)

        def f(g):
            return g.engine_nop()
        P.step("g", f, wait=[P.qlast("g")], inc=("cg", 1))
        ck_kug = P.cnt("cg")

        pE1.close()
        phase_barrier()

        pE2 = contextlib.ExitStack()

        def e2sb(name, shape, dt=F32):
            return pE2.enter_context(nc.sbuf_tensor(name, shape, dt))

        hpt = e2sb("hpt", [128, NI * TOK], BF16)
        wfd = e2sb("wfd", [128, NI * E], BF16)
        yfr = e2sb("yfr", [128, 2 * 512], BF16)
        pE2b = contextlib.ExitStack()

        def e2bsb(name, shape, dt=F32):
            return pE2b.enter_context(nc.sbuf_tensor(name, shape, dt))

        gur = e2bsb("gur", [128, 2 * TOK], BF16)
        fsc = e2bsb("fsc", [128, TOK])
        fsc2 = e2bsb("fsc2", [128, TOK])
        fsb = e2bsb("fsb", [128, 2 * TOK], BF16)

        def f(y):
            last = None
            for ic in range(NI):
                last = y.dma_start(out=wfd[:, ic * E:(ic + 1) * E],
                                   in_=wq_d["down"][:, ic * 128:(ic + 1) * 128],
                                   transpose=True).then_inc(sems["qy"], 16)
            return last
        P.step("y", f, wait=[ck_wf], dma=NI)
        ck_wfd = P.qlast("y")

        h_done = {}
        sig_cs = {}
        gp2_cv = {"g": 0, "s": 0}
        pend_sh = None

        def emit_sh(ic):
            sl2 = ic % 2

            def f(v, ic=ic, sl2=sl2):
                return v.tensor_scalar(fsb[:, sl2 * TOK:sl2 * TOK + TOK],
                                       fsc2[:], MAGIC, 65535.0, AL.subtract,
                                       AL.min)
            P.step("v", f, wait=[("cs", sig_cs[ic])], inc=("cv", 1))
            gp2_cv["s"] = P.cnt("cv")

            def f(v, ic=ic, sl2=sl2):
                v.tensor_tensor(hpt[:, ic * TOK:(ic + 1) * TOK],
                                gur[:, sl2 * TOK:sl2 * TOK + TOK],
                                fsb[:, sl2 * TOK:sl2 * TOK + TOK], AL.mult)
                v.drain()
                return v.tensor_reduce(ffmax[:, ic:ic + 1],
                                       hpt[:, ic * TOK:(ic + 1) * TOK], AX.X,
                                       AL.max, apply_absolute_value=True)
            P.step("v", f, wait=[("cv", P.cnt("cv"))], inc=("cv", 1))
            h_done[ic] = P.cnt("cv")

        for ic in range(NI):
            sl2 = ic % 2

            def f(y, ic=ic, sl2=sl2):
                return y.dma_start(out=gur[:, sl2 * TOK:sl2 * TOK + TOK],
                                   in_=g_d[ic * 128:(ic + 1) * 128, :]
                                   ).then_inc(sems["qy"], 16)
            w = [ck_s1_q]
            if ic >= 2:
                w.append(("cv", h_done[ic - 2]))
            P.step("y", f, wait=w, dma=1)

            def f(s, sl2=sl2):
                return s.activation(fsc[:], gur[:, sl2 * TOK:sl2 * TOK + TOK],
                                    AF.Identity, bias=mgcb[:],
                                    scale=sca[:, 3:4])
            ws = [P.qlast("y"), ("cg", ck_kug), ("cv", gp2_cv["g"])]
            P.step("s", f, wait=ws, inc=("cs", 1))

            def f(v, sl2=sl2):
                return v.tensor_scalar(gur[:, sl2 * TOK:sl2 * TOK + TOK],
                                       fsc[:], MAGIC, 1.0, AL.subtract,
                                       AL.mult)
            P.step("v", f, wait=[("cs", P.cnt("cs"))], inc=("cv", 1))
            gp2_cv["g"] = P.cnt("cv")

            if pend_sh is not None:
                emit_sh(pend_sh)
                pend_sh = None

            def f(s, sl2=sl2):
                s.activation(fsb[:, sl2 * TOK:sl2 * TOK + TOK],
                             gur[:, sl2 * TOK:sl2 * TOK + TOK],
                             AF.Sigmoid, bias=0.0, scale=sca[:, 4:5])
                s.drain()
                return s.activation(fsc2[:],
                                    fsb[:, sl2 * TOK:sl2 * TOK + TOK],
                                    AF.Identity, bias=mgcb[:], scale=65536.0)
            ws = [("cv", gp2_cv["g"]), ("cv", gp2_cv["s"])]
            if ic >= 2:
                ws.append(("cv", h_done[ic - 2]))
            P.step("s", f, wait=ws, inc=("cs", 1))
            sig_cs[ic] = P.cnt("cs")
            pend_sh = ic
        emit_sh(pend_sh)

        def f(v):
            v.tensor_reduce(ffmax[:, 0:1], ffmax[:, 0:NI], AX.X, AL.max)
            return v.drain()
        P.step("v", f, wait=[("cv", P.cnt("cv"))], inc=("cv", 1))
        w_ar = allreduce_max("h", ffmax, 1, 19, [("cv", P.cnt("cv"))])

        def f(v):
            v.tensor_scalar(scr1[0:1, 20:21], scr1[0:1, 23:24], 1.0 / 65536.0,
                            None, AL.mult)
            v.drain()
            v.tensor_tensor(scr1[0:1, 21:22], scr1[0:1, 20:21],
                            scr1[0:1, 19:20], AL.mult)
            v.drain()
            v.tensor_scalar(scr1[0:1, 21:22], scr1[0:1, 21:22], 1.0 / QMAX,
                            1e-12, AL.mult, AL.max)
            v.drain()
            v.reciprocal(scr1[0:1, 22:23], scr1[0:1, 21:22])
            v.drain()
            v.tensor_tensor(scr1[0:1, 22:23], scr1[0:1, 22:23],
                            scr1[0:1, 20:21], AL.mult)
            return v.drain()
        P.step("v", f, wait=w_ar, inc=("cv", 1))

        def f(g):
            g.dma_start(out=bc_d[0:1].rearrange("(o n) -> o n", o=1),
                        in_=scr1[0:1, 22:23]).then_inc(sems["qg"], 16)
            return None
        P.step("g", f, wait=[("cv", P.cnt("cv"))], dma=1)

        def f(g):
            return g.dma_start(out=sca[:, 5:6],
                               in_=bass.AP(bc_t, 0, [[0, 128], [1, 1]])
                               ).then_inc(sems["qg"], 16)
        P.step("g", f, wait=[P.qlast("g")], dma=1)

        def f(g):
            return g.engine_nop()
        P.step("g", f, wait=[P.qlast("g")], inc=("cg", 1))
        ck_kh = P.cnt("cg")
        ck_stage2_v = P.cnt("cv")

        p_done = {}
        for ic in range(NI):
            sl2 = ic % 2

            def f(y, ic=ic, sl2=sl2):
                return y.dma_start(out=gur[:, sl2 * TOK:sl2 * TOK + TOK],
                                   in_=u_d[ic * 128:(ic + 1) * 128, :]
                                   ).then_inc(sems["qy"], 16)
            w = [("cv", ck_stage2_v)]
            if ic >= 2:
                w.append(("cv", p_done[ic - 2]))
            P.step("y", f, wait=w, dma=1)

            def f(s, sl2=sl2):
                return s.activation(fsc[:], gur[:, sl2 * TOK:sl2 * TOK + TOK],
                                    AF.Identity, bias=mgcb[:],
                                    scale=sca[:, 2:3])
            P.step("s", f, wait=[P.qlast("y"), ("cg", ck_kug),
                                 ("cv", gp2_cv["g"])], inc=("cs", 1))

            def f(v, sl2=sl2):
                return v.tensor_scalar(fsb[:, sl2 * TOK:sl2 * TOK + TOK],
                                       fsc[:], MAGIC, 1.0, AL.subtract,
                                       AL.mult)
            P.step("v", f, wait=[("cs", P.cnt("cs"))], inc=("cv", 1))
            gp2_cv["g"] = P.cnt("cv")

            def f(s, ic=ic):
                return s.activation(fsc2[:], hpt[:, ic * TOK:(ic + 1) * TOK],
                                    AF.Identity, bias=mgcb[:],
                                    scale=sca[:, 5:6])
            P.step("s", f, wait=[("cv", gp2_cv["s"]), ("cg", ck_kh),
                                 ("cv", P.cnt("cv"))], inc=("cs", 1))

            def f(v, ic=ic):
                return v.tensor_scalar(hpt[:, ic * TOK:(ic + 1) * TOK],
                                       fsc2[:], MAGIC, 1.0, AL.subtract,
                                       AL.mult)
            P.step("v", f, wait=[("cs", P.cnt("cs"))], inc=("cv", 1))
            gp2_cv["s"] = P.cnt("cv")

            def f(v, ic=ic, sl2=sl2):
                v.tensor_tensor(hpt[:, ic * TOK:(ic + 1) * TOK],
                                hpt[:, ic * TOK:(ic + 1) * TOK],
                                fsb[:, sl2 * TOK:sl2 * TOK + TOK],
                                AL.mult)
                v.drain()
                return v.tensor_reduce(ffmax[:, ic:ic + 1],
                                       hpt[:, ic * TOK:(ic + 1) * TOK], AX.X,
                                       AL.max, apply_absolute_value=True)
            P.step("v", f, wait=[("cv", P.cnt("cv"))], inc=("cv", 1))
            p_done[ic] = P.cnt("cv")

        def f(v):
            v.tensor_reduce(ffmax[:, 0:1], ffmax[:, 0:NI], AX.X, AL.max)
            return v.drain()
        P.step("v", f, wait=[("cv", P.cnt("cv"))], inc=("cv", 1))
        w_ar = allreduce_max("p", ffmax, 1, 19, [("cv", P.cnt("cv"))])

        def f(v):
            v.tensor_tensor(scr1[0:1, 25:26], scr1[0:1, 21:22],
                            scr1[0:1, 17:18], AL.mult)
            v.drain()
            v.reciprocal(scr1[0:1, 26:27], sca[0:1, 2:3])
            v.drain()
            v.tensor_tensor(scr1[0:1, 25:26], scr1[0:1, 25:26],
                            scr1[0:1, 26:27], AL.mult)
            v.drain()
            v.tensor_tensor(scr1[0:1, 27:28], scr1[0:1, 25:26],
                            scr1[0:1, 19:20], AL.mult)
            v.drain()
            v.tensor_scalar(scr1[0:1, 27:28], scr1[0:1, 27:28], 1.0 / QMAX,
                            1e-12, AL.mult, AL.max)
            v.drain()
            v.reciprocal(scr1[0:1, 28:29], scr1[0:1, 27:28])
            v.drain()
            v.tensor_tensor(scr1[0:1, 28:29], scr1[0:1, 28:29],
                            scr1[0:1, 25:26], AL.mult)
            v.drain()
            v.tensor_tensor(scr1[0:1, 29:30], scr1[0:1, 27:28],
                            scr1[0:1, 27:28], AL.mult)
            return v.drain()
        P.step("v", f, wait=w_ar, inc=("cv", 1))

        def f(g):
            g.dma_start(out=bc_d[0:1].rearrange("(o n) -> o n", o=1),
                        in_=scr1[0:1, 28:29]).then_inc(sems["qg"], 16)
            g.dma_start(out=bc_d[1:2].rearrange("(o n) -> o n", o=1),
                        in_=scr1[0:1, 27:28]).then_inc(sems["qg"], 16)
            g.dma_start(out=bc_d[2:3].rearrange("(o n) -> o n", o=1),
                        in_=scr1[0:1, 29:30]).then_inc(sems["qg"], 16)
            return None
        P.step("g", f, wait=[("cv", P.cnt("cv"))], dma=3)

        def f(g):
            g.dma_start(out=sca[:, 6:7],
                        in_=bass.AP(bc_t, 0, [[0, 128], [1, 1]])
                        ).then_inc(sems["qg"], 16)
            g.dma_start(out=sca[:, 7:8],
                        in_=bass.AP(bc_t, 1, [[0, 128], [1, 1]])
                        ).then_inc(sems["qg"], 16)
            g.dma_start(out=sca[:, 8:9],
                        in_=bass.AP(bc_t, 2, [[0, 128], [1, 1]])
                        ).then_inc(sems["qg"], 16)
            return None
        P.step("g", f, wait=[P.qlast("g")], dma=3)

        def f(g):
            return g.engine_nop()
        P.step("g", f, wait=[P.qlast("g")], inc=("cg", 1))
        ck_kp = P.cnt("cg")

        if "sca" in taps:
            def f(g):
                return g.dma_start(out=tap_d["sca"][:, :], in_=sca[:]
                                   ).then_inc(sems["qg"], 16)
            P.step("g", f, wait=[("cg", ck_kp)], dma=1)

        o_done = {}
        for ic in range(NI):
            def f(v, ic=ic):
                v.tensor_scalar(fsc[:], hpt[:, ic * TOK:(ic + 1) * TOK],
                                sca[:, 6:7], MAGIC, AL.mult, AL.add)
                v.drain()
                return v.tensor_scalar(hpt[:, ic * TOK:(ic + 1) * TOK],
                                       fsc[:], MAGIC, 1.0, AL.subtract, AL.mult)
            P.step("v", f, wait=[("cg", ck_kp), ("cv", P.cnt("cv"))],
                   inc=("cv", 1))
            o_done[ic] = P.cnt("cv")
        ck_oall = P.cnt("cv")

        if "hh" in taps:
            def f(g):
                return g.dma_start(out=tap_d["hh"][:, :], in_=hpt[:]
                                   ).then_inc(sems["qg"], 16)
            P.step("g", f, wait=[("cv", ck_oall)], dma=1)

        pE2b.close()
        phase_barrier()

        pF = contextlib.ExitStack()

        def fsb2(name, shape, dt=F32):
            return pF.enter_context(nc.sbuf_tensor(name, shape, dt))

        ydone = {}
        ydw = {}
        ck_yf_qc = {}
        nmd = 0
        pfd = [pE2.enter_context(nc.psum_tensor("pfd%d" % i, [128, 512], F32))
               for i in range(2)]
        for qc in range(NQ5):
            for o in range(8):
                bank = pfd[nmd % 2]

                def f(t, o=o, qc=qc, bank=bank):
                    last = None
                    for ic in range(NI):
                        last = t.matmul(
                            bank[:],
                            wfd[:, ic * E + o * 128:ic * E + o * 128 + 128],
                            hpt[:, ic * TOK + qc * 512:ic * TOK + qc * 512 + 512],
                            start=(ic == 0), stop=(ic == NI - 1))
                    return last
                w = [("cv", ck_oall), ck_wfd]
                if nmd >= 2:
                    w.append(("cs", ydone[nmd - 2]))
                P.step("t", f, wait=w, inc=("ct", 1))

                def f(s, o=o, qc=qc, bank=bank, nmd=nmd):
                    sl = yfr[:, (nmd % 2) * 512:(nmd % 2) * 512 + 512]
                    s.activation(sl, bank[:], AF.Identity, bias=0.0, scale=1.0)
                    s.drain()
                    s.dma_start(
                        out=yf_d[o * 128:(o + 1) * 128, qc * 512:qc * 512 + 512],
                        in_=sl).then_inc(sems["qs"], 16)
                    return None
                ws = [("ct", P.cnt("ct"))]
                if nmd >= 2:
                    ws.append(ydw[nmd - 2])
                if o == 7:
                    op = "qs%d" % (1 - P.qpar.get("s", 0))
                    if P.cnt(op) > 0:
                        ws.append((op, P.cnt(op)))
                P.step("s", f, wait=ws, inc=("cs", 1), dma=1)
                ydone[nmd] = P.cnt("cs")
                ydw[nmd] = P.qlast("s")
                nmd += 1
            ck_yf_qc[qc] = P.qlast("s")
        ck_yf = P.qlast("s")

        if "yy" in taps:
            def f(g):
                return g.dma_start(out=tap_d["yy"][:, :], in_=yf_d[:, :]
                                   ).then_inc(sems["qg"], 16)
            P.step("g", f, wait=[ck_yf], dma=1)

        ytm = fsb2("ytm", [128, 2 * E], BF16)
        x1r2 = fsb2("x1r2", [128, 2 * E])
        otr = fsb2("otr", [128, 2 * E])
        mv2f = fsb2("mv2f", [128, 2])
        st12f = fsb2("st12f", [128, 12])
        rrf = fsb2("rrf", [128, 1])
        mbf = fsb2("mbf", [128, 1])

        ow = {}
        for i in range(NTM):
            sl2 = i % 2

            def f(y, i=i, sl2=sl2):
                y.dma_start(out=ytm[:, sl2 * E:sl2 * E + E],
                            in_=yf_d[:, i * 128:i * 128 + 128],
                            transpose=True).then_inc(sems["qy"], 16)
                return y.dma_start(out=x1r2[:, sl2 * E:sl2 * E + E],
                                   in_=x1_d[i * 128:(i + 1) * 128, :]
                                   ).then_inc(sems["qy"], 16)
            w = [ck_yf_qc[i // 4], ck_x1]
            if i >= 2:
                w.append(ow[i - 2])
            P.step("y", f, wait=w, dma=2)

            def f(v, sl2=sl2):
                a = ytm[:, sl2 * E:sl2 * E + E]
                v.bn_stats(st12f[:, 0:6], a[:, 0:512])
                v.bn_stats(st12f[:, 6:12], a[:, 512:1024])
                v.drain()
                v.bn_aggr(mv2f[:], st12f[:])
                v.drain()
                return v.tensor_scalar(rrf[:], mv2f[:, 1:2], sca[:, 8:9],
                                       EPS, AL.mult, AL.add)
            P.step("v", f, wait=[P.qlast("y")], inc=("cv", 1))

            def f(s):
                return s.activation(rrf[:], rrf[:], AF.Sqrt, bias=0.0,
                                    scale=1.0)
            P.step("s", f, wait=[("cv", P.cnt("cv"))], inc=("cs", 1))

            def f(v):
                v.reciprocal(rrf[:], rrf[:])
                v.drain()
                v.tensor_tensor(rrf[:], rrf[:], sca[:, 7:8], AL.mult)
                v.drain()
                return v.scalar_tensor_tensor(mbf[:], mv2f[:, 0:1], -1.0,
                                              rrf[:], AL.mult, AL.mult)
            P.step("v", f, wait=[("cs", P.cnt("cs"))], inc=("cv", 1))

            def f(s, sl2=sl2):
                return s.activation(otr[:, sl2 * E:sl2 * E + E],
                                    ytm[:, sl2 * E:sl2 * E + E], AF.Identity,
                                    bias=mbf[:], scale=rrf[:])
            P.step("s", f, wait=[("cv", P.cnt("cv"))], inc=("cs", 1))

            def f(v, i=i, sl2=sl2):
                ot = otr[:, sl2 * E:sl2 * E + E]
                return v.tensor_tensor(ot, ot, x1r2[:, sl2 * E:sl2 * E + E],
                                       AL.add)
            P.step("v", f, wait=[("cs", P.cnt("cs"))], inc=("cv", 1))

            def f(g, i=i, sl2=sl2):
                return g.dma_start(out=out_d[i * 128:(i + 1) * 128, :],
                                   in_=otr[:, sl2 * E:sl2 * E + E]
                                   ).then_inc(sems["qg"], 16)
            P.step("g", f, wait=[("cv", P.cnt("cv"))], dma=1)
            ow[i] = P.qlast("g")

        def f(g):
            return g.engine_nop()
        P.step("g", f, wait=[("qg0", P.cnt("qg0")), ("qg1", P.cnt("qg1"))],
               inc=("cg", 1))

        def f(y):
            return None
        P.step("y", f, wait=[("qy0", P.cnt("qy0")), ("qy1", P.cnt("qy1"))])

        def f(s):
            return None
        P.step("s", f, wait=[("qs0", P.cnt("qs0")), ("qs1", P.cnt("qs1"))])

        pF.close()
        pE2.close()

        with nc.Block() as blk:
            for ename in ("y", "v", "s", "t", "g"):
                def make(ename):
                    def emit(eng):
                        lastw = {}
                        for (e, fn, w, inc, semn) in P.steps:
                            if e != ename:
                                continue
                            for (nm, val) in w:
                                if lastw.get(nm, -1) < val:
                                    eng.wait_ge(sems[nm], val)
                                    lastw[nm] = val
                            if semn is not None:
                                sems["q" + ename] = sems[semn]
                            r = fn(eng)
                            if inc is not None:
                                if r is None:
                                    r = eng.nop()
                                r.then_inc(sems[inc[0]], inc[1])
                    return emit
                getattr(blk, ENGMAP[ename])(make(ename))

    return nc


def _prep_inputs(inputs, TOK=2048):
    Wq = np.asarray(inputs["L0_attn_Wq"], dtype=np.float32)
    Wk = np.asarray(inputs["L0_attn_Wk"], dtype=np.float32)
    Wv = np.asarray(inputs["L0_attn_Wv"], dtype=np.float32)
    Wo = np.asarray(inputs["L0_attn_Wo"], dtype=np.float32)
    bo = np.asarray(inputs["L0_attn_bo"], dtype=np.float32)
    wup = np.asarray(inputs["L0_ffn_w_up"], dtype=np.float32)
    wgate = np.asarray(inputs["L0_ffn_w_gate"], dtype=np.float32)
    wdown = np.asarray(inputs["L0_ffn_w_down"], dtype=np.float32)
    x = np.asarray(inputs["L0_attn_x"], dtype=np.float32)

    rowperm = np.array([PAIRS[p][jj] * D + d for p in range(8)
                        for jj in range(2) for d in range(D)], dtype=np.int64)
    wqt = np.ascontiguousarray(Wq[rowperm, :].T)
    wkt = np.ascontiguousarray(Wk.T)
    wvt = np.ascontiguousarray(Wv.T)
    wot = np.ascontiguousarray(Wo[:, rowperm].T)
    bo2 = np.ascontiguousarray(bo.reshape(8, 128).T)

    base = {"wqt": wqt, "wkt": wkt, "wvt": wvt, "wot": wot, "bo": bo2,
            "wup": np.ascontiguousarray(wup),
            "wgate": np.ascontiguousarray(wgate),
            "wdown": np.ascontiguousarray(wdown)}
    in_maps = []
    B = x.shape[0]
    S = x.shape[1]
    for c in range(2 * B):
        b, h = c // 2, c % 2
        if h == 0:
            xs = x[b, 0:TOK, :]
        else:
            xs = x[b, S - 1:S - TOK - 1:-1, :]
        m = dict(base)
        m["x"] = np.ascontiguousarray(xs)
        m["poff"] = np.array([[(1 - h) * 2 * NKV * D * TOK]], dtype=np.int32)
        in_maps.append(m)
    return in_maps


def kernel(L0_attn_x, L0_attn_Wq, L0_attn_Wk, L0_attn_Wv, L0_attn_Wo,
           L0_attn_bo, L0_ffn_w_gate, L0_ffn_w_up, L0_ffn_w_down):
    x = np.asarray(L0_attn_x, dtype=np.float32)
    B, S, _ = x.shape
    TOK = S // 2
    inputs = {"L0_attn_x": x, "L0_attn_Wq": L0_attn_Wq,
              "L0_attn_Wk": L0_attn_Wk, "L0_attn_Wv": L0_attn_Wv,
              "L0_attn_Wo": L0_attn_Wo, "L0_attn_bo": L0_attn_bo,
              "L0_ffn_w_gate": L0_ffn_w_gate, "L0_ffn_w_up": L0_ffn_w_up,
              "L0_ffn_w_down": L0_ffn_w_down}
    in_maps = _prep_inputs(inputs, TOK=TOK)
    nc = build_nc(TOK=TOK)
    res = run_bass_kernel_spmd(nc, in_maps, core_ids=list(range(8)))
    out = np.empty((B, S, E), dtype=np.float32)
    for c in range(2 * B):
        b, h = c // 2, c % 2
        o = np.asarray(res.results[c]["out"])
        if h == 0:
            out[b, 0:TOK, :] = o
        else:
            out[b, TOK:S, :] = o[::-1, :]
    return out
